# version 06
# baseline (speedup 1.0000x reference)
import math
from contextlib import ExitStack

import numpy as np
import ml_dtypes

import concourse.bass as bass
import concourse.mybir as mybir
from concourse.bass_utils import run_bass_kernel_spmd

import concourse.bass_utils as _bu
if not getattr(_bu, "_ldw_patched", False):
    _orig_run_command = _bu.run_command

    def _run_command_ldw(argv, **kw):
        argv = [a.replace("--enable-ldw-opt=false", "--enable-ldw-opt=true")
                if isinstance(a, str) else a for a in argv]
        return _orig_run_command(argv, **kw)

    _bu.run_command = _run_command_ldw
    _bu._ldw_patched = True

F32 = mybir.dt.float32
BF16 = mybir.dt.bfloat16
AF = mybir.ActivationFunctionType
ALU = mybir.AluOpType
AX = mybir.AxisListType
BF = ml_dtypes.bfloat16

B, S, D = 4, 4096, 4096
N = B * S
NCORE = 8
NT = N // NCORE
HD = 128
NQH, NKV, GSZ = 32, 8, 4
KVD = NKV * HD
KEXP, RANK, TOPK = 64, 16, 16
SCALING0 = 32.0 / 16.0
E1, R1 = 4, 8
NTT = NT // 128
ATT_SCALE = 1.0 / math.sqrt(HD)
ATT_SHIFT = -20.0
EPS = 1e-5

ENGS = ("sync", "gpsimd", "scalar", "vector", "tensor")
CHAINED = {"scalar": True, "vector": True, "gpsimd": True, "tensor": False, "sync": False}



class Em:

    def __init__(self, carry=None):
        self.q = {e: [] for e in ENGS}
        self.cnt = dict(carry) if carry else {}
        self.sem_names = set()

    def _bump(self, sem, k):
        v = self.cnt.get(sem, 0) + k
        self.cnt[sem] = v
        self.sem_names.add(sem)
        return v

    def wait(self, eng, milestone):
        sem, val = milestone
        if val > 0:
            self.q[eng].append(("wait", sem, val))

    def op(self, eng, meth, *a, after=(), inc=False, sem_override=None, **kw):
        for m in after:
            if m is not None:
                self.wait(eng, m)
        s = "s_" + eng
        ms = None
        incs = []
        if sem_override is not None:
            incs.append((sem_override, 1))
            ms = (sem_override, self._bump(sem_override, 1))
        elif CHAINED[eng]:
            n = self.cnt.get(s, 0)
            if n > 0:
                self.q[eng].append(("wait", s, n))
            incs.append((s, 1))
            ms = (s, self._bump(s, 1))
        elif inc:
            incs.append((s, 1))
            ms = (s, self._bump(s, 1))
        self.q[eng].append(("op", meth, a, kw, incs))
        return ms

    def dma(self, eng, out, in_, after=(), **kw):
        for m in after:
            if m is not None:
                self.wait(eng, m)
        d = "d_" + eng
        ms = (d, self._bump(d, 16))
        self.q[eng].append(("op", "dma_start", (), dict(out=out, in_=in_, **kw),
                            [(d, 16)]))
        return ms

    def raw(self, eng, meth, *a, **kw):
        self.q[eng].append(("op", meth, a, kw, []))

    def playback(self, nc, ctx, sems):
        block = ctx.enter_context(nc.Block())
        for eng in ENGS:
            items = self.q[eng]
            if not items:
                continue

            def body(e, items=items):
                for it in items:
                    if it[0] == "wait":
                        e.wait_ge(sems[it[1]], it[2])
                    else:
                        _, meth, a, kw, incs = it
                        inst = getattr(e, meth)(*a, **kw)
                        for (sname, k) in incs:
                            inst.then_inc(sems[sname], k)
            getattr(block, eng)(body)
        return sems



def _chunk_lhsT(w):
    k, m = w.shape
    return np.ascontiguousarray(
        w.reshape(k // 128, 128, m // 128, 128).transpose(2, 1, 0, 3)
        .reshape(m // 128, 128, k))


def _consts():
    ident = np.eye(128, dtype=BF)
    Rm = np.zeros((128, 128), np.float32)
    for i in range(64):
        Rm[i, i + 64] = -1.0
        Rm[i + 64, i] = 1.0
    rT = np.ascontiguousarray(Rm.T).astype(BF)
    inv_freq = 1.0 / (10000.0 ** (np.arange(0, HD, 2, dtype=np.float64) / HD))
    ang = np.arange(S, dtype=np.float64)[:, None] * inv_freq[None, :]
    ang2 = np.concatenate([ang, ang], axis=-1)
    cosT = np.ascontiguousarray(np.cos(ang2).T).astype(np.float32)
    sinT = np.ascontiguousarray(np.sin(ang2).T).astype(np.float32)
    mv = np.zeros((4, 128, 512), np.float32)
    for d in range(4):
        for j in range(4):
            if j < d:
                mv[d, :, j * 128:(j + 1) * 128] = -1e9
            elif j == d:
                blk = np.where(np.arange(128)[:, None] > np.arange(128)[None, :],
                               np.float32(-1e9), np.float32(0.0))
                mv[d, :, j * 128:(j + 1) * 128] = blk
    maskvar = np.ascontiguousarray(mv.astype(BF))
    e64 = np.zeros((64, 1024), np.float32)
    for e in range(64):
        e64[e, e * 16:(e + 1) * 16] = 1.0
    e4 = np.zeros((4, 32), np.float32)
    for e in range(4):
        e4[e, e * 8:(e + 1) * 8] = 1.0
    ones_col = np.ones((128, 1), np.float32)
    ones_row = np.ones((1, 128), np.float32)
    return dict(ident=ident, rT=rT, cosT=cosT, sinT=sinT, maskvar=maskvar,
                e64=np.ascontiguousarray(e64.astype(BF)),
                e4=np.ascontiguousarray(e4.astype(BF)),
                ones_col=ones_col, ones_row=ones_row)



def emit_ln_to_xnt(nc, ctx, em, x_ap, xnt, ident_sb, tag="ln"):
    E = ctx.enter_context
    xt = [E(nc.sbuf_tensor(f"{tag}xt{i}", [128, D], F32)) for i in range(2)]
    xn = [E(nc.sbuf_tensor(f"{tag}xn{i}", [128, D], BF16)) for i in range(2)]
    st = E(nc.sbuf_tensor(f"{tag}st", [128, 8], F32))
    tp = [E(nc.psum_tensor(f"{tag}tp{i}", [128, 128], BF16)) for i in range(4)]

    ev_ms = {}
    ap_ms = {}
    sq_ms = {}
    tr_ms = {}
    for tt in range(NTT):
        dep = [ap_ms.get(tt - 2), sq_ms.get(tt - 2)]
        m_dma = em.dma("sync", out=xt[tt % 2][:, :],
                       in_=x_ap[tt * 128:(tt + 1) * 128, :], after=dep)
        m_sq = em.op("scalar", "activation", xn[tt % 2][:, :], xt[tt % 2][:, :],
                     AF.Square, accum_out=st[:, 1:2],
                     after=[m_dma, tr_ms.get((tt - 2, 31))])
        sq_ms[tt] = m_sq
        em.op("vector", "reduce_sum", st[:, 0:1], xt[tt % 2][:, :], axis=AX.X,
              after=[m_dma])
        em.op("vector", "tensor_scalar_mul", st[:, 2:3], st[:, 0:1], 1.0 / D)
        em.op("vector", "tensor_scalar_mul", st[:, 3:4], st[:, 1:2], 1.0 / D,
              after=[m_sq])
        em.op("vector", "tensor_mul", st[:, 7:8], st[:, 2:3], st[:, 2:3])
        em.op("vector", "tensor_sub", st[:, 4:5], st[:, 3:4], st[:, 7:8])
        m_var = em.op("vector", "tensor_scalar_add", st[:, 4:5], st[:, 4:5], EPS)
        em.op("scalar", "activation", st[:, 5:6], st[:, 4:5], AF.Ln, after=[m_var])
        m_rstd = em.op("scalar", "activation", st[:, 6:7], st[:, 5:6], AF.Exp,
                       scale=-0.5)
        ap_ms[tt] = em.op("vector", "tensor_scalar", xn[tt % 2][:, :],
                          xt[tt % 2][:, :], st[:, 2:3], st[:, 6:7],
                          op0=ALU.subtract, op1=ALU.mult,
                          after=[m_rstd, tr_ms.get((tt - 2, 31))])
        for c in range(32):
            prev = (tt, c - 4) if c >= 4 else (tt - 1, c + 28)
            m_tr = em.op("tensor", "transpose", tp[c % 4][:, :],
                         xn[tt % 2][:, c * 128:(c + 1) * 128], ident_sb[:, :],
                         after=[ap_ms[tt], ev_ms.get(prev)], inc=True)
            tr_ms[(tt, c)] = m_tr
            ev_ms[(tt, c)] = em.op("vector", "tensor_copy",
                                   xnt[:, c * NT + tt * 128: c * NT + (tt + 1) * 128],
                                   tp[c % 4][:, :], after=[m_tr])
    return ev_ms[(NTT - 1, 31)]


def emit_featmajor_proj(nc, ctx, em, w_ap, chunks, xnt, evac_cb, tag,
                        kdim=D, ntok=NT, ready=None, dma_eng="gpsimd", pj=None):
    E = ctx.enter_context
    nk = kdim // 128
    nqb = ntok // 512
    slab = [E(nc.sbuf_tensor(f"{tag}slab{i}", [128, nk * 128], BF16)) for i in range(2)]
    if pj is None:
        pj = [E(nc.psum_tensor(f"{tag}pj{i}", [128, 512], F32)) for i in range(2)]
    mm_ms = {}
    free_ms = {}
    dw_ms = {}
    for ci, c in enumerate(chunks):
        dep = [mm_ms.get((ci - 2, nqb - 1)), ready if ci == 0 else None]
        dw_ms[ci] = em.dma(dma_eng, out=slab[ci % 2][:, :], in_=w_ap[c],
                           after=dep)
        for qb in range(nqb):
            u = (ci * nqb + qb) % 2
            pu = ci * nqb + qb - 2
            deps = [dw_ms[ci] if qb == 0 else None,
                    free_ms.get(divmod(pu, nqb)) if pu >= 0 else None]
            if ci == 0 and qb == 0 and ready is not None:
                deps.append(ready)
            for d in range(nk):
                last = d == nk - 1
                mm = em.op("tensor", "matmul", pj[u][:, :],
                           slab[ci % 2][:, d * 128:(d + 1) * 128],
                           xnt[:, d * ntok + qb * 512: d * ntok + qb * 512 + 512],
                           start=(d == 0), stop=last,
                           after=deps if d == 0 else (), inc=last)
            mm_ms[(ci, qb)] = mm
            free_ms[(ci, qb)] = evac_cb(c, qb, pj[u], mm)
    return mm_ms, free_ms



def build_A(core):
    nc = bass.Bass("TRN2", target_bir_lowering=False)
    ap = lambda t: t.ap()
    x = nc.dram_tensor("x", [NT, D], F32, kind="ExternalInput").ap()
    wq = nc.dram_tensor("wq", [32, 128, D], F32, kind="ExternalInput").ap()
    wk = nc.dram_tensor("wk", [8, 128, D], F32, kind="ExternalInput").ap()
    wv = nc.dram_tensor("wv", [D, KVD], F32, kind="ExternalInput").ap()
    wrt = nc.dram_tensor("wrt", [D, KEXP], F32, kind="ExternalInput").ap()
    aall = nc.dram_tensor("aall", [8, 128, D], F32, kind="ExternalInput").ap()
    cost = nc.dram_tensor("cost", [128, NT], F32, kind="ExternalInput").ap()
    sint = nc.dram_tensor("sint", [128, NT], F32, kind="ExternalInput").ap()
    ident = nc.dram_tensor("ident", [128, 128], BF16, kind="ExternalInput").ap()
    rtm = nc.dram_tensor("rtm", [128, 128], BF16, kind="ExternalInput").ap()
    e64 = nc.dram_tensor("e64", [64, 1024], BF16, kind="ExternalInput").ap()

    q_t = nc.dram_tensor("q_t", [D, NT], BF16, kind="ExternalOutput").ap()
    k_t = nc.dram_tensor("k_t", [KVD, NT], BF16, kind="ExternalOutput").ap()
    v_tok = nc.dram_tensor("v_tok", [NT, KVD], BF16, kind="ExternalOutput").ap()
    hg_t = nc.dram_tensor("hg_t", [KEXP * RANK, NT], BF16, kind="ExternalOutput").ap()

    with ExitStack() as ctx:
        E = ctx.enter_context
        sems = {n: E(nc.semaphore(n)) for n in
                ("s_sync", "s_gpsimd", "s_scalar", "s_vector", "s_tensor",
                 "d_sync", "d_gpsimd", "d_scalar")}
        xnt = E(nc.sbuf_tensor("xnt", [128, 32 * NT], BF16))
        ident_sb = E(nc.sbuf_tensor("ident_sb", [128, 128], BF16))
        rt_sb = E(nc.sbuf_tensor("rt_sb", [128, 128], BF16))
        cos_sb = E(nc.sbuf_tensor("cos_sb", [128, NT], F32))
        sin_sb = E(nc.sbuf_tensor("sin_sb", [128, NT], F32))

        with ExitStack() as c1:
            em = Em()
            em.dma("gpsimd", out=ident_sb[:, :], in_=ident)
            em.dma("gpsimd", out=cos_sb[:, :], in_=cost)
            em.dma("gpsimd", out=sin_sb[:, :], in_=sint)
            m_cst = em.dma("gpsimd", out=rt_sb[:, :], in_=rtm)
            em.wait("tensor", m_cst)
            emit_ln_to_xnt(nc, c1, em, x, xnt, ident_sb)
            em.playback(nc, c1, sems)

        with ExitStack() as c2:
            E2 = c2.enter_context
            em = Em(carry=em.cnt)
            stage = [E2(nc.sbuf_tensor(f"qst{i}", [128, NT], BF16)) for i in range(2)]
            qraw = [E2(nc.sbuf_tensor(f"qraw{i}", [128, 512], BF16)) for i in range(2)]
            t1 = [E2(nc.sbuf_tensor(f"ropet1_{i}", [128, 512], F32)) for i in range(2)]
            rot = [E2(nc.psum_tensor(f"rope_rot{i}", [128, 512], F32)) for i in range(2)]
            st_out_ms = {}
            rope_add = {}
            rope_mm = {}

            def evac_qk(c, qb, ps, m_mm):
                lin = c * 4 + qb
                dma_guard = st_out_ms.get(c - 2) if qb == 0 else None
                m_qraw = em.op("scalar", "copy", qraw[qb % 2][:, :], ps[:, :],
                               after=[m_mm, rope_add.get(lin - 2),
                                      rope_mm.get(lin - 2), dma_guard])
                m_rot = em.op("tensor", "matmul", rot[qb % 2][:, :], rt_sb[:, :],
                              qraw[qb % 2][:, :], start=True, stop=True,
                              after=[m_qraw, rope_add.get(lin - 2)], inc=True)
                rope_mm[lin] = m_rot
                em.op("vector", "tensor_tensor", t1[qb % 2][:, :], qraw[qb % 2][:, :],
                      cos_sb[:, qb * 512:(qb + 1) * 512], op=ALU.mult, after=[m_qraw])
                em.op("vector", "scalar_tensor_tensor",
                      stage[c % 2][:, qb * 512:(qb + 1) * 512],
                      rot[qb % 2][:, :], 1.0, sin_sb[:, qb * 512:(qb + 1) * 512],
                      op0=ALU.mult, op1=ALU.mult, after=[m_rot])
                m_add = em.op("vector", "tensor_tensor",
                              stage[c % 2][:, qb * 512:(qb + 1) * 512],
                              stage[c % 2][:, qb * 512:(qb + 1) * 512],
                              t1[qb % 2][:, :], op=ALU.add)
                rope_add[lin] = m_add
                if qb == 3:
                    dst = q_t[c * 128:(c + 1) * 128, :] if c < 32 else \
                        k_t[(c - 32) * 128:(c - 31) * 128, :]
                    st_out_ms[c] = em.dma("sync", out=dst, in_=stage[c % 2][:, :],
                                          after=[m_add])
                return m_qraw

            emit_featmajor_proj(nc, c2, em, wq, list(range(32)), xnt, evac_qk, "wq")

            def evac_k(c, qb, ps, m_mm):
                return evac_qk(c + 32, qb, ps, m_mm)
            emit_featmajor_proj(nc, c2, em, wk, list(range(8)), xnt, evac_k, "wk")
            em.playback(nc, c2, sems)

        with ExitStack() as c3:
            E3 = c3.enter_context
            em = Em(carry=em.cnt)
            wv_sb = E3(nc.sbuf_tensor("wv_sb", [128, 32 * 512], BF16))
            vstage = [E3(nc.sbuf_tensor(f"vst{i}", [128, 512], BF16)) for i in range(2)]
            pv = [E3(nc.psum_tensor(f"pv{i}", [128, 512], F32)) for i in range(2)]
            last_mm = None
            for half in range(2):
                m_w = em.dma("gpsimd",
                             out=wv_sb.ap().rearrange("p (d j) -> p d j", j=512),
                             in_=wv[:, half * 512:(half + 1) * 512]
                             .rearrange("(d p) j -> p d j", p=128),
                             after=[last_mm])
                ev = {}
                for tt in range(NTT):
                    u = tt % 2
                    for d in range(32):
                        mm = em.op("tensor", "matmul", pv[u][:, :],
                                   xnt[:, d * NT + tt * 128: d * NT + (tt + 1) * 128],
                                   wv_sb[:, d * 512:(d + 1) * 512],
                                   start=(d == 0), stop=(d == 31),
                                   after=([m_w, ev.get(tt - 2)] if d == 0 else ()),
                                   inc=(d == 31))
                    last_mm = mm
                    m_cp = em.op("scalar", "copy", vstage[u][:, :], pv[u][:, :],
                                 after=[mm])
                    m_out = em.dma("sync", out=v_tok[tt * 128:(tt + 1) * 128,
                                                    half * 512:(half + 1) * 512],
                                   in_=vstage[u][:, :], after=[m_cp])
                    ev[tt] = m_out
            em.playback(nc, c3, sems)

        with ExitStack() as c4:
            E4 = c4.enter_context
            em = Em(carry=em.cnt)
            wr_sb = E4(nc.sbuf_tensor("wr_sb", [128, 32 * 64], BF16))
            e64_sb = E4(nc.sbuf_tensor("e64_sb", [64, 1024], BF16))
            sc = E4(nc.sbuf_tensor("sc", [128, 64], F32))
            work = E4(nc.sbuf_tensor("work", [128, 64], F32))
            eqm = E4(nc.sbuf_tensor("eqm", [128, 64], F32))
            ex = E4(nc.sbuf_tensor("ex", [128, 64], F32))
            mx = E4(nc.sbuf_tensor("mx", [128, 18], F32))
            gtok = E4(nc.sbuf_tensor("gtok", [128, 64], BF16))
            gt_sb = E4(nc.sbuf_tensor("gt_sb", [64, NT], BF16))
            gexp = E4(nc.sbuf_tensor("gexp", [128, 8 * NT], BF16))
            hstage = [E4(nc.sbuf_tensor(f"hst{i}", [128, 512], BF16)) for i in range(2)]
            psc = E4(nc.psum_tensor("psc", [128, 64], F32))
            pgt = E4(nc.psum_tensor("pgt", [64, 128], BF16))
            pge = [E4(nc.psum_tensor(f"pge{i}", [128, 512], F32)) for i in range(2)]

            m_wr = em.dma("gpsimd",
                          out=wr_sb.ap().rearrange("p (d e) -> p d e", e=64),
                          in_=wrt.rearrange("(d p) e -> p d e", p=128))
            m_e64 = em.dma("gpsimd", out=e64_sb[:, :], in_=e64)

            gt_ms = {}
            for tt in range(NTT):
                for d in range(32):
                    mm = em.op("tensor", "matmul", psc[:, :],
                               xnt[:, d * NT + tt * 128: d * NT + (tt + 1) * 128],
                               wr_sb[:, d * 64:(d + 1) * 64],
                               start=(d == 0), stop=(d == 31),
                               after=([m_wr, gt_ms.get(tt - 1)] if d == 0 else ()),
                               inc=(d == 31))
                m_sc = em.op("vector", "tensor_copy", sc[:, :], psc[:, :], after=[mm])
                em.op("vector", "tensor_copy", work[:, :], sc[:, :])
                em.op("vector", "reduce_max", mx[:, 0:1], work[:, :], axis=AX.X)
                for i in range(1, TOPK):
                    em.op("vector", "tensor_single_scalar", eqm[:, :], work[:, :],
                          mx[:, i - 1:i], op=ALU.is_equal)
                    em.op("vector", "scalar_tensor_tensor", work[:, :], eqm[:, :],
                          -1e30, work[:, :], op0=ALU.mult, op1=ALU.add)
                    em.op("vector", "reduce_max", mx[:, i:i + 1], work[:, :], axis=AX.X)
                em.op("vector", "tensor_single_scalar", eqm[:, :], sc[:, :],
                      mx[:, TOPK - 1:TOPK], op=ALU.is_ge)
                m_negm = em.op("vector", "tensor_scalar_mul", mx[:, 16:17],
                               mx[:, 0:1], -1.0)
                m_exp = em.op("scalar", "activation", ex[:, :], sc[:, :], AF.Exp,
                              bias=mx[:, 16:17], after=[m_negm])
                em.op("vector", "tensor_mul", ex[:, :], ex[:, :], eqm[:, :],
                      after=[m_exp])
                em.op("vector", "reduce_sum", mx[:, 17:18], ex[:, :], axis=AX.X)
                em.op("vector", "reciprocal", mx[:, 17:18], mx[:, 17:18])
                m_g = em.op("vector", "tensor_scalar", gtok[:, :], ex[:, :],
                            mx[:, 17:18], SCALING0, op0=ALU.mult, op1=ALU.mult)
                m_tr = em.op("tensor", "transpose", pgt[:, :], gtok[:, :],
                             ident_sb[:, :], after=[m_g], inc=True)
                gt_ms[tt] = em.op("vector", "tensor_copy",
                                  gt_sb[0:64, tt * 128:(tt + 1) * 128], pgt[:, :],
                                  after=[m_tr])
            ge_ms = {}
            for ec in range(8):
                for qb in range(4):
                    u = (ec * 4 + qb) % 2
                    mm = em.op("tensor", "matmul", pge[u][:, :],
                               e64_sb[:, ec * 128:(ec + 1) * 128],
                               gt_sb[0:64, qb * 512:(qb + 1) * 512],
                               start=True, stop=True,
                               after=[m_e64, gt_ms[NTT - 1],
                                      ge_ms.get((ec * 4 + qb) - 2)], inc=True)
                    ge_ms[ec * 4 + qb] = em.op(
                        "vector", "tensor_copy",
                        gexp[:, ec * NT + qb * 512: ec * NT + qb * 512 + 512],
                        pge[u][:, :], after=[mm])
            m_gexp = ge_ms[31]

            hout_ms = {}

            def evac_h(c, qb, ps, m_mm):
                lin = c * 4 + qb
                u = lin % 2
                m_ev = em.op("vector", "tensor_tensor", hstage[u][:, :], ps[:, :],
                             gexp[:, c * NT + qb * 512: c * NT + qb * 512 + 512],
                             op=ALU.mult, after=[m_mm, m_gexp,
                                                 hout_ms.get(lin - 2)])
                hout_ms[lin] = em.dma("sync",
                                      out=hg_t[c * 128:(c + 1) * 128,
                                               qb * 512:(qb + 1) * 512],
                                      in_=hstage[u][:, :], after=[m_ev])
                return m_ev

            emit_featmajor_proj(nc, c4, em, aall, list(range(8)), xnt, evac_h, "ha")
            for dn in ("d_sync", "d_gpsimd", "d_scalar"):
                if em.cnt.get(dn, 0) > 0:
                    eng = dn[2:]
                    em.wait(eng, (dn, em.cnt[dn]))
            em.playback(nc, c4, sems)

    return nc




def build_attn(causal):
    nc = bass.Bass("TRN2", target_bir_lowering=False)
    qt_in = nc.dram_tensor("qt_in", [2048, S], BF16, kind="ExternalInput").ap()
    kt_in = nc.dram_tensor("kt_in", [512, S], BF16, kind="ExternalInput").ap()
    v_in = nc.dram_tensor("v_in", [S, 512], BF16, kind="ExternalInput").ap()
    ident = nc.dram_tensor("ident", [128, 128], BF16, kind="ExternalInput").ap()
    mv = nc.dram_tensor("mv", [4, 128, 512], BF16, kind="ExternalInput").ap()
    onec = nc.dram_tensor("onec", [128, 1], F32, kind="ExternalInput").ap()
    oner = nc.dram_tensor("oner", [1, 128], F32, kind="ExternalInput").ap()
    ctx_t = nc.dram_tensor("ctx_t", [2048, S], BF16, kind="ExternalOutput").ap()

    NKT = S // 128
    NQBJ = S // 512

    with ExitStack() as ctx:
        E = ctx.enter_context
        sems = {n: E(nc.semaphore(n)) for n in
                ("s_sync", "s_gpsimd", "s_scalar", "s_vector", "s_tensor",
                 "d_sync", "d_gpsimd", "d_scalar")}
        kg = [E(nc.sbuf_tensor(f"kg{i}", [128, S], BF16)) for i in range(2)]
        vg = [E(nc.sbuf_tensor(f"vg{i}", [128, NKT * 128], BF16)) for i in range(2)]
        qh = [E(nc.sbuf_tensor(f"qh{i}", [128, S], BF16)) for i in range(2)]
        ident_sb = E(nc.sbuf_tensor("ident_sb", [128, 128], BF16))
        mv_sb = E(nc.sbuf_tensor("mv_sb", [128, 4 * 512], BF16))
        onec_sb = E(nc.sbuf_tensor("onec_sb", [128, 1], F32))
        oner_sb = E(nc.sbuf_tensor("oner_sb", [1, 128], F32))
        shift_sb = E(nc.sbuf_tensor("shift_sb", [128, 1], F32))
        p_sb = [E(nc.sbuf_tensor(f"p{i}", [128, 512], BF16)) for i in range(2)]
        acc = E(nc.sbuf_tensor("acc", [128, 512], F32))
        rsig = E(nc.sbuf_tensor("rsig", [1, 512], F32))
        r_sb = E(nc.sbuf_tensor("r_sb", [128, 512], F32))
        cst = [E(nc.sbuf_tensor(f"cst{i}", [128, 512], BF16)) for i in range(2)]
        ps_s = [E(nc.psum_tensor(f"ps_s{i}", [128, 512], F32)) for i in range(2)]
        ps_c = E(nc.psum_tensor("ps_c", [128, 512], F32))
        ps_sig = E(nc.psum_tensor("ps_sig", [1, 512], F32))
        ps_r = E(nc.psum_tensor("ps_r", [128, 512], F32))

        em = Em()
        em.dma("gpsimd", out=ident_sb[:, :], in_=ident)
        em.dma("gpsimd", out=mv_sb.ap().rearrange("p (d j) -> p d j", j=512),
               in_=mv.rearrange("d p j -> p d j"))
        em.dma("gpsimd", out=onec_sb[:, :], in_=onec)
        m_cst = em.dma("gpsimd", out=oner_sb[:, :], in_=oner)
        m_shift = em.op("vector", "memset", shift_sb[:, :], ATT_SHIFT)

        kload = {}
        vload = {}
        qload = {}
        last_k_use = {}
        last_v_use = {}
        last_q_use = {}
        evac_ms = {}
        out_ms = {}
        sig_ms = {}
        rcp_ms = {}

        unit = 0
        for g in range(4):
            kload[g] = em.dma("sync", out=kg[g % 2][:, :],
                              in_=kt_in[g * 128:(g + 1) * 128, :],
                              after=[last_k_use.get(g - 2)])
            vload[g] = em.dma("sync",
                              out=vg[g % 2].ap().rearrange("p (t d) -> p t d", d=128),
                              in_=v_in[:, g * 128:(g + 1) * 128]
                              .rearrange("(t p) d -> p t d", p=128),
                              after=[last_v_use.get(g - 2)])
            for h in range(4):
                gh = g * 4 + h
                qload[gh] = em.dma("sync", out=qh[gh % 2][:, :],
                                   in_=qt_in[gh * 128:(gh + 1) * 128, :],
                                   after=[last_q_use.get(gh - 2)])
                for qb in range(NQBJ):
                    qt0 = qb * 4
                    kts = list(range(qt0 + 4)) if causal else list(range(NKT))
                    nk = len(kts)
                    m_ms = em.op("vector", "memset", acc[:, :], 0.0,
                                 after=[sig_ms.get(unit - 1)])
                    exp_ms = {}
                    mm_ms = {}
                    av_last = None
                    for i in range(nk + 1):
                        if i < nk:
                            kt = kts[i]
                            u = i % 2
                            deps = [exp_ms.get(i - 2)]
                            if i == 0:
                                deps += [kload[g], qload[gh], m_shift]
                            boundary = causal and kt >= qt0
                            mm = em.op("tensor", "matmul", ps_s[u][:, :],
                                       kg[g % 2][:, kt * 128:(kt + 1) * 128],
                                       qh[gh % 2][:, qb * 512:(qb + 1) * 512],
                                       start=True, stop=not boundary,
                                       after=deps, inc=not boundary)
                            if boundary:
                                d = kt - qt0
                                mm = em.op("tensor", "matmul", ps_s[u][:, :],
                                           ident_sb[:, :],
                                           mv_sb[:, d * 512:(d + 1) * 512],
                                           start=False, stop=True, inc=True)
                            mm_ms[i] = mm
                            exp_ms[i] = em.op("scalar", "activation", p_sb[u][:, :],
                                              ps_s[u][:, :], AF.Exp,
                                              bias=shift_sb[:, :], scale=ATT_SCALE,
                                              after=[mm])
                        if i >= 1:
                            j = i - 1
                            u = j % 2
                            deps = [exp_ms[j]]
                            if j == 0:
                                deps += [evac_ms.get(unit - 1), vload[g]]
                            av = em.op("tensor", "matmul", ps_c[:, :],
                                       vg[g % 2][:, kts[j] * 128:(kts[j] + 1) * 128],
                                       p_sb[u][:, :],
                                       start=(j == 0), stop=(j == nk - 1),
                                       after=deps, inc=(j == nk - 1))
                            av_last = av
                            em.op("vector", "tensor_tensor", acc[:, :], acc[:, :],
                                  p_sb[u][:, :], op=ALU.add, after=[exp_ms[j]])
                    last_k_use[g] = mm_ms[nk - 1]
                    last_q_use[gh] = mm_ms[nk - 1]
                    last_v_use[g] = av_last
                    m_sig = em.op("tensor", "matmul", ps_sig[:, :], onec_sb[:, :],
                                  acc[:, :], start=True, stop=True,
                                  after=[rcp_ms.get(unit - 1)], inc=True)
                    sig_ms[unit] = m_sig
                    m_rec = em.op("vector", "reciprocal", rsig[:, :], ps_sig[:, :],
                                  after=[m_sig])
                    m_rmm = em.op("tensor", "matmul", ps_r[:, :], oner_sb[:, :],
                                  rsig[:, :], start=True, stop=True,
                                  after=[m_rec], inc=True)
                    rcp_ms[unit] = em.op("vector", "tensor_copy", r_sb[:, :],
                                         ps_r[:, :], after=[m_rmm])
                    m_ev = em.op("vector", "tensor_tensor", cst[unit % 2][:, :],
                                 ps_c[:, :], r_sb[:, :], op=ALU.mult,
                                 after=[av_last, out_ms.get(unit - 2)])
                    evac_ms[unit] = m_ev
                    out_ms[unit] = em.dma("sync",
                                          out=ctx_t[gh * 128:(gh + 1) * 128,
                                                    qb * 512:(qb + 1) * 512],
                                          in_=cst[unit % 2][:, :], after=[m_ev])
                    unit += 1
        for dn in ("d_sync", "d_gpsimd"):
            if em.cnt.get(dn, 0) > 0:
                em.wait(dn[2:], (dn, em.cnt[dn]))
        em.playback(nc, ctx, sems)
    return nc


def run_attn(ii, layer, qT, kT, v_tok):
    nc = build_attn(causal=(layer == 0))
    in_maps = []
    for c in range(NCORE):
        b, half = c // 2, c % 2
        g0 = 4 * half
        in_maps.append(dict(
            qt_in=np.ascontiguousarray(qT[b][g0 * 512:(g0 + 4) * 512]),
            kt_in=np.ascontiguousarray(kT[b][g0 * 128:(g0 + 4) * 128]),
            v_in=np.ascontiguousarray(v_tok[b][:, g0 * 128:(g0 + 4) * 128]),
            ident=ii["ident"], mv=ii["maskvar"],
            onec=ii["ones_col"], oner=ii["ones_row"],
        ))
    res = run_bass_kernel_spmd(nc, in_maps, core_ids=list(range(NCORE)))
    ctxT = []
    for b in range(B):
        ctxT.append(np.concatenate(
            [res.results[2 * b]["ctx_t"], res.results[2 * b + 1]["ctx_t"]], axis=0))
    return np.stack(ctxT)




def emit_back_sum(nc, ctx, em, streams, out_tok, ident_sb, bo_sb=None, tag="bk"):
    E = ctx.enter_context
    slabs = []
    for si, (w_ap, nk, src_sb) in enumerate(streams):
        kparts = 32 if nk == "z32" else 128
        nkc = 1 if nk == "z32" else nk
        slabs.append([E(nc.sbuf_tensor(f"{tag}sl{si}_{i}", [kparts, nkc * 128], BF16))
                      for i in range(2)])
    pj = [E(nc.psum_tensor(f"{tag}pj{i}", [128, 512], F32)) for i in range(2)]
    tp = [E(nc.psum_tensor(f"{tag}tp{i}", [128, 128], BF16)) for i in range(2)]
    sumbf = [E(nc.sbuf_tensor(f"{tag}sb{i}", [128, 512], BF16)) for i in range(2)]
    stok = [E(nc.sbuf_tensor(f"{tag}st{i}", [128, 128], BF16)) for i in range(2)]

    dw_ms = {}
    mm_ms = {}
    ev_ms = {}
    cp_ms = {}
    do_ms = {}
    tr_ms = {}
    for c in range(32):
        deps = [mm_ms.get((c - 2, 3))]
        for si, (w_ap, nk, src_sb) in enumerate(streams):
            if nk == "z32":
                dw_ms[(c, si)] = em.dma("gpsimd", out=slabs[si][c % 2][:, :],
                                        in_=w_ap[c], after=deps)
            else:
                dw_ms[(c, si)] = em.dma("gpsimd", out=slabs[si][c % 2][:, :],
                                        in_=w_ap[c], after=deps)
        nstream = len(streams)
        for qb in range(4):
            u = (c * 4 + qb) % 2
            lin = c * 4 + qb
            first = True
            for si, (w_ap, nk, src_sb) in enumerate(streams):
                nkc = 1 if nk == "z32" else nk
                for d in range(nkc):
                    deps2 = []
                    if first:
                        deps2 = [dw_ms[(c, s)] for s in range(nstream)]
                        deps2.append(ev_ms.get(lin - 2))
                    last = (si == nstream - 1) and (d == nkc - 1)
                    if nk == "z32":
                        rhs = src_sb[0:32, qb * 512:(qb + 1) * 512]
                        lhsT = slabs[si][c % 2][:, :]
                    else:
                        rhs = src_sb[:, d * NT + qb * 512: d * NT + qb * 512 + 512]
                        lhsT = slabs[si][c % 2][:, d * 128:(d + 1) * 128]
                    mm = em.op("tensor", "matmul", pj[u][:, :], lhsT, rhs,
                               start=first, stop=last, after=deps2, inc=last)
                    first = False
            mm_ms[(c, qb)] = mm
            if bo_sb is None:
                ev_ms[lin] = em.op("scalar", "copy", sumbf[u][:, :], pj[u][:, :],
                                   after=[mm])
            else:
                ev_ms[lin] = em.op("scalar", "add", sumbf[u][:, :], pj[u][:, :],
                                   bo_sb[:, c:c + 1], after=[mm])
            for j in range(4):
                tu = (lin * 4 + j) % 2
                m_tr = em.op("tensor", "transpose", tp[tu][:, :],
                             sumbf[u][:, j * 128:(j + 1) * 128], ident_sb[:, :],
                             after=[ev_ms[lin], cp_ms.get((lin * 4 + j) - 2)],
                             inc=True)
                tr_ms[lin] = m_tr
                m_cp = em.op("vector", "tensor_copy", stok[tu][:, :], tp[tu][:, :],
                             after=[m_tr, do_ms.get((lin * 4 + j) - 2)])
                cp_ms[lin * 4 + j] = m_cp
                tt = qb * 4 + j
                do_ms[lin * 4 + j] = em.dma(
                    "sync", out=out_tok[tt * 128:(tt + 1) * 128, c * 128:(c + 1) * 128],
                    in_=stok[tu][:, :], after=[m_cp])
    return do_ms[32 * 4 * 4 - 1]


def emit_residual(nc, ctx, em, out_ap, parts, tag="res"):
    E = ctx.enter_context
    bufs = [[E(nc.sbuf_tensor(f"{tag}b{pi}_{i}", [128, D], F32 if pi == 0 else BF16))
             for i in range(2)] for pi in range(len(parts))]
    ob = [E(nc.sbuf_tensor(f"{tag}o{i}", [128, D], F32)) for i in range(2)]
    add_ms = {}
    out_ms = {}
    for tt in range(NTT):
        u = tt % 2
        lms = []
        for pi, pap in enumerate(parts):
            lms.append(em.dma("sync", out=bufs[pi][u][:, :],
                              in_=pap[tt * 128:(tt + 1) * 128, :],
                              after=[add_ms.get(tt - 2)]))
        m = em.op("vector", "tensor_tensor", ob[u][:, :], bufs[0][u][:, :],
                  bufs[1][u][:, :], op=ALU.add, after=lms + [out_ms.get(tt - 2)])
        for pi in range(2, len(parts)):
            m = em.op("vector", "tensor_tensor", ob[u][:, :], ob[u][:, :],
                      bufs[pi][u][:, :], op=ALU.add)
        add_ms[tt] = m
        out_ms[tt] = em.dma("sync", out=out_ap[tt * 128:(tt + 1) * 128, :],
                            in_=ob[u][:, :], after=[m])
    return out_ms[NTT - 1]


def build_C():
    nc = bass.Bass("TRN2", target_bir_lowering=False)
    x = nc.dram_tensor("x", [NT, D], F32, kind="ExternalInput").ap()
    ctxt = nc.dram_tensor("ctxt", [D, NT], BF16, kind="ExternalInput").ap()
    hgt = nc.dram_tensor("hgt", [KEXP * RANK, NT], BF16, kind="ExternalInput").ap()
    wo = nc.dram_tensor("wo", [32, 128, D], F32, kind="ExternalInput").ap()
    ball = nc.dram_tensor("ball", [32, 128, KEXP * RANK], F32, kind="ExternalInput").ap()
    ident = nc.dram_tensor("ident", [128, 128], BF16, kind="ExternalInput").ap()
    x1 = nc.dram_tensor("x1", [NT, D], F32, kind="ExternalOutput").ap()
    sum_tok = nc.dram_tensor("sum_tok", [NT, D], BF16).ap()

    with ExitStack() as ctx:
        E = ctx.enter_context
        sems = {n: E(nc.semaphore(n)) for n in
                ("s_sync", "s_gpsimd", "s_scalar", "s_vector", "s_tensor",
                 "d_sync", "d_gpsimd", "d_scalar")}
        ident_sb = E(nc.sbuf_tensor("ident_sb", [128, 128], BF16))
        with ExitStack() as c1:
            ctx_sb = c1.enter_context(nc.sbuf_tensor("ctx_sb", [128, 32 * NT], BF16))
            hg_sb = c1.enter_context(nc.sbuf_tensor("hg_sb", [128, 8 * NT], BF16))
            em = Em()
            em.dma("gpsimd", out=ident_sb[:, :], in_=ident)
            m1 = em.dma("gpsimd", out=ctx_sb.ap().rearrange("p (d n) -> p d n", n=NT),
                        in_=ctxt.rearrange("(d p) n -> p d n", p=128))
            m2 = em.dma("gpsimd", out=hg_sb.ap().rearrange("p (d n) -> p d n", n=NT),
                        in_=hgt.rearrange("(d p) n -> p d n", p=128))
            em.wait("tensor", m1)
            em.wait("tensor", m2)
            emit_back_sum(nc, c1, em,
                          [(wo, 32, ctx_sb), (ball, 8, hg_sb)],
                          sum_tok, ident_sb, tag="c1")
            em.wait("sync", ("d_sync", em.cnt["d_sync"]))
            em.playback(nc, c1, sems)
        with ExitStack() as c2:
            em = Em(carry=em.cnt)
            emit_residual(nc, c2, em, x1, [x, sum_tok], tag="c2")
            em.wait("sync", ("d_sync", em.cnt["d_sync"]))
            em.playback(nc, c2, sems)
    return nc


def build_F():
    nc = bass.Bass("TRN2", target_bir_lowering=False)
    x1 = nc.dram_tensor("x1", [NT, D], F32, kind="ExternalInput").ap()
    ctxt = nc.dram_tensor("ctxt", [D, NT], BF16, kind="ExternalInput").ap()
    xnt_in = nc.dram_tensor("xnt_in", [D, NT], BF16, kind="ExternalInput").ap()
    ztt = nc.dram_tensor("ztt", [E1 * R1, NT], BF16, kind="ExternalInput").ap()
    wo = nc.dram_tensor("wo", [32, 128, D], F32, kind="ExternalInput").ap()
    wup = nc.dram_tensor("wup", [32, 128, D], F32, kind="ExternalInput").ap()
    lb = nc.dram_tensor("lb", [32, 32, 128], F32, kind="ExternalInput").ap()
    bo = nc.dram_tensor("bo", [128, 32], F32, kind="ExternalInput").ap()
    ident = nc.dram_tensor("ident", [128, 128], BF16, kind="ExternalInput").ap()
    out = nc.dram_tensor("out", [NT, D], F32, kind="ExternalOutput").ap()
    sum1_tok = nc.dram_tensor("sum1_tok", [NT, D], BF16).ap()
    sum2_tok = nc.dram_tensor("sum2_tok", [NT, D], BF16).ap()

    with ExitStack() as ctx:
        E = ctx.enter_context
        sems = {n: E(nc.semaphore(n)) for n in
                ("s_sync", "s_gpsimd", "s_scalar", "s_vector", "s_tensor",
                 "d_sync", "d_gpsimd", "d_scalar")}
        ident_sb = E(nc.sbuf_tensor("ident_sb", [128, 128], BF16))
        bo_sb = E(nc.sbuf_tensor("bo_sb", [128, 32], F32))
        with ExitStack() as f1:
            big_sb = f1.enter_context(nc.sbuf_tensor("big_sb", [128, 32 * NT], BF16))
            em = Em()
            em.dma("gpsimd", out=ident_sb[:, :], in_=ident)
            em.dma("gpsimd", out=bo_sb[:, :], in_=bo)
            m1 = em.dma("gpsimd", out=big_sb.ap().rearrange("p (d n) -> p d n", n=NT),
                        in_=ctxt.rearrange("(d p) n -> p d n", p=128))
            em.wait("tensor", m1)
            emit_back_sum(nc, f1, em, [(wo, 32, big_sb)], sum1_tok, ident_sb,
                          bo_sb=bo_sb, tag="f1")
            em.wait("sync", ("d_sync", em.cnt["d_sync"]))
            em.playback(nc, f1, sems)
        with ExitStack() as f2:
            big2_sb = f2.enter_context(nc.sbuf_tensor("big2_sb", [128, 32 * NT], BF16))
            zt_sb = f2.enter_context(nc.sbuf_tensor("zt_sb", [32, NT], BF16))
            em = Em(carry=em.cnt)
            m1 = em.dma("gpsimd", out=big2_sb.ap().rearrange("p (d n) -> p d n", n=NT),
                        in_=xnt_in.rearrange("(d p) n -> p d n", p=128))
            m2 = em.dma("gpsimd", out=zt_sb[:, :], in_=ztt)
            em.wait("tensor", m1)
            em.wait("tensor", m2)
            emit_back_sum(nc, f2, em, [(wup, 32, big2_sb), (lb, "z32", zt_sb)],
                          sum2_tok, ident_sb, tag="f2")
            em.wait("sync", ("d_sync", em.cnt["d_sync"]))
            em.playback(nc, f2, sems)
        with ExitStack() as f3:
            em = Em(carry=em.cnt)
            emit_residual(nc, f3, em, out, [x1, sum1_tok, sum2_tok], tag="f3")
            em.wait("sync", ("d_sync", em.cnt["d_sync"]))
            em.playback(nc, f3, sems)
    return nc




def build_D():
    nc = bass.Bass("TRN2", target_bir_lowering=False)
    x = nc.dram_tensor("x", [NT, D], F32, kind="ExternalInput").ap()
    wq = nc.dram_tensor("wq", [32, 128, D], F32, kind="ExternalInput").ap()
    wk = nc.dram_tensor("wk", [8, 128, D], F32, kind="ExternalInput").ap()
    wv = nc.dram_tensor("wv", [D, KVD], F32, kind="ExternalInput").ap()
    gwt = nc.dram_tensor("gwt", [D, E1], F32, kind="ExternalInput").ap()
    ebias = nc.dram_tensor("ebias", [1, E1], F32, kind="ExternalInput").ap()
    la = nc.dram_tensor("la", [D, E1 * R1], F32, kind="ExternalInput").ap()
    e4 = nc.dram_tensor("e4", [E1, E1 * R1], BF16, kind="ExternalInput").ap()
    ident = nc.dram_tensor("ident", [128, 128], BF16, kind="ExternalInput").ap()
    oner = nc.dram_tensor("oner", [1, 128], F32, kind="ExternalInput").ap()

    q_t = nc.dram_tensor("q_t", [D, NT], BF16, kind="ExternalOutput").ap()
    k_t = nc.dram_tensor("k_t", [KVD, NT], BF16, kind="ExternalOutput").ap()
    v_tok = nc.dram_tensor("v_tok", [NT, KVD], BF16, kind="ExternalOutput").ap()
    zt_t = nc.dram_tensor("zt_t", [E1 * R1, NT], BF16, kind="ExternalOutput").ap()
    xn_t = nc.dram_tensor("xn_t", [D, NT], BF16, kind="ExternalOutput").ap()

    with ExitStack() as ctx:
        E = ctx.enter_context
        sems = {n: E(nc.semaphore(n)) for n in
                ("s_sync", "s_gpsimd", "s_scalar", "s_vector", "s_tensor",
                 "d_sync", "d_gpsimd", "d_scalar")}
        xnt = E(nc.sbuf_tensor("xnt", [128, 32 * NT], BF16))
        ident_sb = E(nc.sbuf_tensor("ident_sb", [128, 128], BF16))
        oner_sb = E(nc.sbuf_tensor("oner_sb", [1, 128], F32))

        with ExitStack() as c1:
            em = Em()
            em.dma("gpsimd", out=oner_sb[:, :], in_=oner)
            m_cst = em.dma("gpsimd", out=ident_sb[:, :], in_=ident)
            em.wait("tensor", m_cst)
            emit_ln_to_xnt(nc, c1, em, x, xnt, ident_sb)
            em.playback(nc, c1, sems)

        with ExitStack() as c2:
            E2 = c2.enter_context
            em = Em(carry=em.cnt)
            stage = [E2(nc.sbuf_tensor(f"qst{i}", [128, NT], BF16)) for i in range(2)]
            st_out_ms = {}
            cp_ms = {}

            def evac_qk(c, qb, ps, m_mm):
                lin = c * 4 + qb
                dma_guard = st_out_ms.get(c - 2) if qb == 0 else None
                m_cp = em.op("scalar", "copy", stage[c % 2][:, qb * 512:(qb + 1) * 512],
                             ps[:, :], after=[m_mm, dma_guard])
                cp_ms[lin] = m_cp
                if qb == 3:
                    dst = q_t[c * 128:(c + 1) * 128, :] if c < 32 else \
                        k_t[(c - 32) * 128:(c - 31) * 128, :]
                    st_out_ms[c] = em.dma("sync", out=dst, in_=stage[c % 2][:, :],
                                          after=[m_cp])
                return m_cp

            emit_featmajor_proj(nc, c2, em, wq, list(range(32)), xnt, evac_qk, "wq")

            def evac_k(c, qb, ps, m_mm):
                return evac_qk(c + 32, qb, ps, m_mm)
            emit_featmajor_proj(nc, c2, em, wk, list(range(8)), xnt, evac_k, "wk")
            em.playback(nc, c2, sems)

        with ExitStack() as c3:
            E3 = c3.enter_context
            em = Em(carry=em.cnt)
            wv_sb = E3(nc.sbuf_tensor("wv_sb", [128, 32 * 512], BF16))
            vstage = [E3(nc.sbuf_tensor(f"vst{i}", [128, 512], BF16)) for i in range(2)]
            pv = [E3(nc.psum_tensor(f"pv{i}", [128, 512], F32)) for i in range(2)]
            last_mm = None
            for half in range(2):
                m_w = em.dma("gpsimd",
                             out=wv_sb.ap().rearrange("p (d j) -> p d j", j=512),
                             in_=wv[:, half * 512:(half + 1) * 512]
                             .rearrange("(d p) j -> p d j", p=128),
                             after=[last_mm])
                ev = {}
                for tt in range(NTT):
                    u = tt % 2
                    for d in range(32):
                        mm = em.op("tensor", "matmul", pv[u][:, :],
                                   xnt[:, d * NT + tt * 128: d * NT + (tt + 1) * 128],
                                   wv_sb[:, d * 512:(d + 1) * 512],
                                   start=(d == 0), stop=(d == 31),
                                   after=([m_w, ev.get(tt - 2)] if d == 0 else ()),
                                   inc=(d == 31))
                    last_mm = mm
                    m_cp = em.op("scalar", "copy", vstage[u][:, :], pv[u][:, :],
                                 after=[mm])
                    ev[tt] = em.dma("sync", out=v_tok[tt * 128:(tt + 1) * 128,
                                                     half * 512:(half + 1) * 512],
                                    in_=vstage[u][:, :], after=[m_cp])
            em.playback(nc, c3, sems)

        with ExitStack() as c4:
            E4 = c4.enter_context
            em = Em(carry=em.cnt)
            gw_sb = E4(nc.sbuf_tensor("gw_sb", [128, 32 * E1], BF16))
            eb_sb = E4(nc.sbuf_tensor("eb_sb", [1, E1], F32))
            la_sb = E4(nc.sbuf_tensor("la_sb", [128, 32 * 32], BF16))
            e4_sb = E4(nc.sbuf_tensor("e4_sb", [E1, E1 * R1], BF16))
            lg = E4(nc.sbuf_tensor("lg", [128, E1], F32))
            mxv = E4(nc.sbuf_tensor("mxv", [128, 1], F32))
            oh = E4(nc.sbuf_tensor("oh", [128, E1], BF16))
            oht = E4(nc.sbuf_tensor("oht", [E1, NT], BF16))
            ztsb = E4(nc.sbuf_tensor("ztsb", [32, NT], BF16))
            ztg = [E4(nc.sbuf_tensor(f"ztg{i}", [32, 512], BF16)) for i in range(2)]
            plg = E4(nc.psum_tensor("plg", [128, E1], F32))
            poh = E4(nc.psum_tensor("poh", [E1, 128], BF16))
            pzt = [E4(nc.psum_tensor(f"pzt{i}", [32, 512], F32)) for i in range(2)]
            pg4 = [E4(nc.psum_tensor(f"pg4{i}", [32, 512], F32)) for i in range(2)]

            m_gw = em.dma("gpsimd",
                          out=gw_sb.ap().rearrange("p (d e) -> p d e", e=E1),
                          in_=gwt.rearrange("(d p) e -> p d e", p=128))
            m_eb = em.dma("gpsimd", out=eb_sb[:, :], in_=ebias)
            m_la = em.dma("gpsimd",
                          out=la_sb.ap().rearrange("p (d e) -> p d e", e=32),
                          in_=la.rearrange("(d p) e -> p d e", p=128))
            m_e4 = em.dma("gpsimd", out=e4_sb[:, :], in_=e4)

            for cchunk in range(32):
                em.dma("sync", out=xn_t[cchunk * 128:(cchunk + 1) * 128, :],
                       in_=xnt[:, cchunk * NT:(cchunk + 1) * NT])

            oht_ms = {}
            for tt in range(NTT):
                for d in range(32):
                    mm = em.op("tensor", "matmul", plg[:, :],
                               xnt[:, d * NT + tt * 128: d * NT + (tt + 1) * 128],
                               gw_sb[:, d * E1:(d + 1) * E1],
                               start=(d == 0), stop=False,
                               after=([m_gw, m_eb, oht_ms.get(tt - 1)] if d == 0 else ()))
                mm = em.op("tensor", "matmul", plg[:, :], oner_sb[:, :], eb_sb[:, :],
                           start=False, stop=True, inc=True)
                m_lg = em.op("vector", "tensor_copy", lg[:, :], plg[:, :], after=[mm])
                em.op("vector", "reduce_max", mxv[:, :], lg[:, :], axis=AX.X)
                m_oh = em.op("vector", "tensor_single_scalar", oh[:, :], lg[:, :],
                             mxv[:, :], op=ALU.is_equal)
                m_tr = em.op("tensor", "transpose", poh[:, :], oh[:, :],
                             ident_sb[:, :], after=[m_oh, oht_ms.get(tt - 1)],
                             inc=True)
                oht_ms[tt] = em.op("vector", "tensor_copy",
                                   oht[0:E1, tt * 128:(tt + 1) * 128], poh[:, :],
                                   after=[m_tr])
            zt_ms = {}
            for qb in range(4):
                u = qb % 2
                for d in range(32):
                    mm = em.op("tensor", "matmul", pzt[u][:, :],
                               la_sb[:, d * 32:(d + 1) * 32].transpose([0, 1]),
                               xnt[:, d * NT + qb * 512: d * NT + qb * 512 + 512],
                               start=(d == 0), stop=(d == 31),
                               after=([m_la, zt_ms.get(qb - 2)] if d == 0 else ()),
                               inc=(d == 31))
                m_cp = em.op("scalar", "copy", ztsb[0:32, qb * 512:(qb + 1) * 512],
                             pzt[u][:, :], after=[mm])
                mmg = em.op("tensor", "matmul", pg4[u][:, :], e4_sb[:, :],
                            oht[0:E1, qb * 512:(qb + 1) * 512],
                            start=True, stop=True,
                            after=[m_e4, oht_ms[NTT - 1], zt_ms.get(qb - 2)],
                            inc=True)
                m_ztg = em.op("vector", "tensor_tensor", ztg[u][:, :], pg4[u][:, :],
                              ztsb[0:32, qb * 512:(qb + 1) * 512], op=ALU.mult,
                              after=[mmg, m_cp])
                zt_ms[qb] = em.dma("sync", out=zt_t[:, qb * 512:(qb + 1) * 512],
                                   in_=ztg[u][:, :], after=[m_ztg])
            for dn in ("d_sync", "d_gpsimd"):
                if em.cnt.get(dn, 0) > 0:
                    em.wait(dn[2:], (dn, em.cnt[dn]))
            em.playback(nc, c4, sems)
    return nc


def run_D(ii, x1):
    nc = build_D()
    in_maps = []
    for c in range(NCORE):
        in_maps.append(dict(
            x=x1[c * NT:(c + 1) * NT],
            wq=ii["wq1"], wk=ii["wk1"], wv=ii["wv1"],
            gwt=ii["gwt1"], ebias=ii["ebias1"], la=ii["la1p"], e4=ii["e4"],
            ident=ii["ident"], oner=ii["ones_row"],
        ))
    res = run_bass_kernel_spmd(nc, in_maps, core_ids=list(range(NCORE)))
    return res.results




BASE_T = ((0, 3), (1, 2))


def stripe_tiles(half):
    return [4 * p + BASE_T[half][r] for p in range(8) for r in range(2)]


def _mask_table(consts, half):
    mv, S512 = consts["maskvar"], 512
    zeros = np.zeros((128, S512), BF)
    full = np.full((128, S512), np.float32(-1e9)).astype(BF)

    def mfull(nf):
        m = np.zeros((128, S512), np.float32)
        m[:, :nf * 128] = -1e9
        return m.astype(BF)

    G = [BASE_T[half][r] + 4 * p for p in range(2) for r in range(2)]
    out = []
    for dd in range(8):
        if dd in G:
            out.append(np.asarray(mv[G.index(dd)]))
        else:
            nf = sum(1 for x in G if x < dd)
            if nf == 0:
                out.append(zeros)
            elif nf >= 4:
                out.append(full)
            else:
                out.append(mfull(nf))
    return np.ascontiguousarray(np.stack(out))


def _emit_qkv_rope(nc, ctx, em, wq, wk, xnt, q_dst, k_dst, cos_sb, sin_sb, rt_sb,
                   rope, tag):
    E = ctx.enter_context
    stage = [E(nc.sbuf_tensor(f"{tag}qst{i}", [128, NT], BF16)) for i in range(2)]
    st_out_ms = {}
    if rope:
        qraw = [E(nc.sbuf_tensor(f"{tag}qraw{i}", [128, 512], BF16)) for i in range(2)]
        t1 = [E(nc.sbuf_tensor(f"{tag}t1_{i}", [128, 512], F32)) for i in range(2)]
        rot = [E(nc.psum_tensor(f"{tag}rot{i}", [128, 512], F32)) for i in range(2)]
        rope_add = {}
        rope_mm = {}

        def evac(c, qb, ps, m_mm):
            lin = c * 4 + qb
            dma_guard = st_out_ms.get(c - 2) if qb == 0 else None
            m_qraw = em.op("scalar", "copy", qraw[qb % 2][:, :], ps[:, :],
                           after=[m_mm, rope_add.get(lin - 2), rope_mm.get(lin - 2),
                                  dma_guard])
            m_rot = em.op("tensor", "matmul", rot[qb % 2][:, :], rt_sb[:, :],
                          qraw[qb % 2][:, :], start=True, stop=True,
                          after=[m_qraw, rope_add.get(lin - 2)], inc=True)
            rope_mm[lin] = m_rot
            em.op("vector", "tensor_tensor", t1[qb % 2][:, :], qraw[qb % 2][:, :],
                  cos_sb[:, qb * 512:(qb + 1) * 512], op=ALU.mult, after=[m_qraw])
            em.op("vector", "scalar_tensor_tensor",
                  stage[c % 2][:, qb * 512:(qb + 1) * 512],
                  rot[qb % 2][:, :], 1.0, sin_sb[:, qb * 512:(qb + 1) * 512],
                  op0=ALU.mult, op1=ALU.mult, after=[m_rot])
            m_add = em.op("vector", "tensor_tensor",
                          stage[c % 2][:, qb * 512:(qb + 1) * 512],
                          stage[c % 2][:, qb * 512:(qb + 1) * 512],
                          t1[qb % 2][:, :], op=ALU.add)
            rope_add[lin] = m_add
            if qb == 3:
                dst = q_dst[c * 128:(c + 1) * 128, :] if c < 32 else \
                    k_dst[(c - 32) * 128:(c - 31) * 128, :]
                st_out_ms[c] = em.dma("sync", out=dst, in_=stage[c % 2][:, :],
                                      after=[m_add])
            return m_qraw
    else:
        def evac(c, qb, ps, m_mm):
            dma_guard = st_out_ms.get(c - 2) if qb == 0 else None
            m_cp = em.op("scalar", "copy", stage[c % 2][:, qb * 512:(qb + 1) * 512],
                         ps[:, :], after=[m_mm, dma_guard])
            if qb == 3:
                dst = q_dst[c * 128:(c + 1) * 128, :] if c < 32 else \
                    k_dst[(c - 32) * 128:(c - 31) * 128, :]
                st_out_ms[c] = em.dma("sync", out=dst, in_=stage[c % 2][:, :],
                                      after=[m_cp])
            return m_cp

    pj_sh = [ctx.enter_context(nc.psum_tensor(f"{tag}pj{i}", [128, 512], F32))
             for i in range(2)]
    _, wq_free = emit_featmajor_proj(nc, ctx, em, wq, list(range(32)), xnt, evac,
                                     tag + "wq", pj=pj_sh)

    def evac_k(c, qb, ps, m_mm):
        return evac(c + 32, qb, ps, m_mm)
    emit_featmajor_proj(nc, ctx, em, wk, list(range(8)), xnt, evac_k, tag + "wk",
                        pj=pj_sh, ready=wq_free[(31, 3)])
    return st_out_ms


def _emit_vproj(nc, ctx, em, wv, xnt, v_dst, tag):
    E = ctx.enter_context
    wv_sb = E(nc.sbuf_tensor(f"{tag}wv_sb", [128, 32 * 512], BF16))
    vstage = [E(nc.sbuf_tensor(f"{tag}vst{i}", [128, 512], BF16)) for i in range(2)]
    pv = [E(nc.psum_tensor(f"{tag}pv{i}", [128, 512], F32)) for i in range(2)]
    last_mm = None
    for half in range(2):
        m_w = em.dma("gpsimd",
                     out=wv_sb.ap().rearrange("p (d j) -> p d j", j=512),
                     in_=wv[:, half * 512:(half + 1) * 512]
                     .rearrange("(d p) j -> p d j", p=128), after=[last_mm])
        ev = {}
        for tt in range(NTT):
            u = tt % 2
            for d in range(32):
                mm = em.op("tensor", "matmul", pv[u][:, :],
                           xnt[:, d * NT + tt * 128: d * NT + (tt + 1) * 128],
                           wv_sb[:, d * 512:(d + 1) * 512],
                           start=(d == 0), stop=(d == 31),
                           after=([m_w, ev.get(tt - 2)] if d == 0 else ()),
                           inc=(d == 31))
            last_mm = mm
            m_cp = em.op("scalar", "copy", vstage[u][:, :], pv[u][:, :], after=[mm])
            ev[tt] = em.dma("sync", out=v_dst[tt * 128:(tt + 1) * 128,
                                             half * 512:(half + 1) * 512],
                            in_=vstage[u][:, :], after=[m_cp])
    return last_mm


def _emit_attn_fused(nc, ctx, em, causal, q_src, agk, agv, ctx_sb,
                     ident_sb, mall_sb, onec_sb, oner_sb, shift_sb, tag):
    E = ctx.enter_context
    kg = [E(nc.sbuf_tensor(f"{tag}kg{i}", [128, S], BF16)) for i in range(2)]
    vg = [E(nc.sbuf_tensor(f"{tag}vg{i}", [128, 32 * 128], BF16)) for i in range(2)]
    qh = [E(nc.sbuf_tensor(f"{tag}qh{i}", [128, NT], BF16)) for i in range(2)]
    p_sb = [E(nc.sbuf_tensor(f"{tag}p{i}", [128, 512], BF16)) for i in range(2)]
    acc = E(nc.sbuf_tensor(f"{tag}acc", [128, 512], F32))
    rsig = E(nc.sbuf_tensor(f"{tag}rsig", [1, 512], F32))
    r_sb = E(nc.sbuf_tensor(f"{tag}r_sb", [128, 512], F32))
    ps_s = [E(nc.psum_tensor(f"{tag}ps_s{i}", [128, 512], F32)) for i in range(2)]
    ps_c = E(nc.psum_tensor(f"{tag}ps_c", [128, 512], F32))
    ps_sig = E(nc.psum_tensor(f"{tag}ps_sig", [1, 512], F32))
    ps_r = E(nc.psum_tensor(f"{tag}ps_r", [128, 512], F32))

    kload, vload, qload = {}, {}, {}
    last_k_use, last_v_use, last_q_use = {}, {}, {}
    evac_ms, sig_ms, rcp_ms = {}, {}, {}
    unit = 0
    for g in range(8):
        for rh in range(2):
            for rr in range(2):
                base = BASE_T[rh][rr]
                km = em.dma(
                    "sync",
                    out=kg[g % 2].ap().rearrange("k (p four c) -> k p four c",
                                                 four=4, c=128)[:, :, base:base + 1, :],
                    in_=agk[rh * 1024 + g * 128: rh * 1024 + (g + 1) * 128, :]
                    .rearrange("k (p two c) -> k p two c", two=2, c=128)
                    [:, :, rr:rr + 1, :],
                    after=[last_k_use.get(g - 2)] if (rh == 0 and rr == 0) else ())
                vm = em.dma(
                    "gpsimd",
                    out=vg[g % 2].ap().rearrange("k (p four d) -> k p four d",
                                                 four=4, d=128)[:, :, base:base + 1, :],
                    in_=agv[rh * NT: (rh + 1) * NT, g * 128:(g + 1) * 128]
                    .rearrange("(p two k) d -> k p two d", two=2, k=128)
                    [:, :, rr:rr + 1, :],
                    after=[last_v_use.get(g - 2)] if (rh == 0 and rr == 0) else ())
        kload[g] = km
        vload[g] = vm
        for h in range(4):
            H = g * 4 + h
            gh = H
            qload[gh] = em.dma("sync", out=qh[gh % 2][:, :],
                               in_=q_src[H * 128:(H + 1) * 128, :],
                               after=[last_q_use.get(gh - 2)])
            for qb in range(4):
                kts = list(range(min(32, 8 * qb + 8))) if causal else list(range(32))
                nk = len(kts)
                m_ms = em.op("vector", "memset", acc[:, :], 0.0,
                             after=[sig_ms.get(unit - 1)])
                exp_ms = {}
                mm_ms = {}
                av_last = None
                for i in range(nk + 1):
                    if i < nk:
                        kt = kts[i]
                        u = i % 2
                        deps = [exp_ms.get(i - 2)]
                        if i == 0:
                            deps += [kload[g], qload[gh], m_ms]
                        dd = kt - 8 * qb
                        masked = causal and dd >= 0
                        mm = em.op("tensor", "matmul", ps_s[u][:, :],
                                   kg[g % 2][:, kt * 128:(kt + 1) * 128],
                                   qh[gh % 2][:, qb * 512:(qb + 1) * 512],
                                   start=True, stop=not masked,
                                   after=deps, inc=not masked)
                        if masked:
                            mm = em.op("tensor", "matmul", ps_s[u][:, :],
                                       ident_sb[:, :],
                                       mall_sb[:, dd * 512:(dd + 1) * 512],
                                       start=False, stop=True, inc=True)
                        mm_ms[i] = mm
                        exp_ms[i] = em.op("scalar", "activation", p_sb[u][:, :],
                                          ps_s[u][:, :], AF.Exp,
                                          bias=shift_sb[:, :], scale=ATT_SCALE,
                                          after=[mm])
                    if i >= 1:
                        j = i - 1
                        u = j % 2
                        deps = [exp_ms[j]]
                        if j == 0:
                            deps += [evac_ms.get(unit - 1), vload[g]]
                        av = em.op("tensor", "matmul", ps_c[:, :],
                                   vg[g % 2][:, kts[j] * 128:(kts[j] + 1) * 128],
                                   p_sb[u][:, :],
                                   start=(j == 0), stop=(j == nk - 1),
                                   after=deps, inc=(j == nk - 1))
                        av_last = av
                        em.op("vector", "tensor_tensor", acc[:, :], acc[:, :],
                              p_sb[u][:, :], op=ALU.add, after=[exp_ms[j]])
                last_k_use[g] = mm_ms[nk - 1]
                last_q_use[gh] = mm_ms[nk - 1]
                last_v_use[g] = av_last
                m_sig = em.op("tensor", "matmul", ps_sig[:, :], onec_sb[:, :],
                              acc[:, :], start=True, stop=True,
                              after=[rcp_ms.get(unit - 1)], inc=True)
                sig_ms[unit] = m_sig
                m_rec = em.op("vector", "reciprocal", rsig[:, :], ps_sig[:, :],
                              after=[m_sig])
                m_rmm = em.op("tensor", "matmul", ps_r[:, :], oner_sb[:, :],
                              rsig[:, :], start=True, stop=True,
                              after=[m_rec], inc=True)
                rcp_ms[unit] = em.op("vector", "tensor_copy", r_sb[:, :],
                                     ps_r[:, :], after=[m_rmm])
                evac_ms[unit] = em.op("vector", "tensor_tensor",
                                      ctx_sb[:, H * NT + qb * 512:
                                             H * NT + qb * 512 + 512],
                                      ps_c[:, :], r_sb[:, :], op=ALU.mult,
                                      after=[av_last])
                unit += 1
    return evac_ms[unit - 1]


def _emit_gates_l0(nc, ctx, em, wrt, e64, xnt, hg_dst, aall, ident_sb, tag="g0"):
    E = ctx.enter_context
    wr_sb = E(nc.sbuf_tensor(f"{tag}wr", [128, 32 * 64], BF16))
    e64_sb = E(nc.sbuf_tensor(f"{tag}e64", [64, 1024], BF16))
    sc = E(nc.sbuf_tensor(f"{tag}sc", [128, 64], F32))
    work = E(nc.sbuf_tensor(f"{tag}work", [128, 64], F32))
    eqm = E(nc.sbuf_tensor(f"{tag}eqm", [128, 64], F32))
    ex = E(nc.sbuf_tensor(f"{tag}ex", [128, 64], F32))
    mx = E(nc.sbuf_tensor(f"{tag}mx", [128, 18], F32))
    gtok = E(nc.sbuf_tensor(f"{tag}gtok", [128, 64], BF16))
    gt_sb = E(nc.sbuf_tensor(f"{tag}gt", [64, NT], BF16))
    gexp = E(nc.sbuf_tensor(f"{tag}gexp", [128, 8 * NT], BF16))
    hstage = [E(nc.sbuf_tensor(f"{tag}hst{i}", [128, 512], BF16)) for i in range(2)]
    psc = E(nc.psum_tensor(f"{tag}psc", [128, 64], F32))
    pgt = E(nc.psum_tensor(f"{tag}pgt", [64, 128], BF16))
    pge = [E(nc.psum_tensor(f"{tag}pge0", [128, 512], F32))]

    m_wr = em.dma("gpsimd", out=wr_sb.ap().rearrange("p (d e) -> p d e", e=64),
                  in_=wrt.rearrange("(d p) e -> p d e", p=128))
    m_e64 = em.dma("gpsimd", out=e64_sb[:, :], in_=e64)
    gt_ms = {}
    for tt in range(NTT):
        for d in range(32):
            mm = em.op("tensor", "matmul", psc[:, :],
                       xnt[:, d * NT + tt * 128: d * NT + (tt + 1) * 128],
                       wr_sb[:, d * 64:(d + 1) * 64],
                       start=(d == 0), stop=(d == 31),
                       after=([m_wr, gt_ms.get(tt - 1)] if d == 0 else ()),
                       inc=(d == 31))
        m_sc = em.op("vector", "tensor_copy", sc[:, :], psc[:, :], after=[mm])
        em.op("vector", "tensor_copy", work[:, :], sc[:, :])
        em.op("vector", "reduce_max", mx[:, 0:1], work[:, :], axis=AX.X)
        for i in range(1, TOPK):
            em.op("vector", "tensor_single_scalar", eqm[:, :], work[:, :],
                  mx[:, i - 1:i], op=ALU.is_equal)
            em.op("vector", "scalar_tensor_tensor", work[:, :], eqm[:, :],
                  -1e30, work[:, :], op0=ALU.mult, op1=ALU.add)
            em.op("vector", "reduce_max", mx[:, i:i + 1], work[:, :], axis=AX.X)
        em.op("vector", "tensor_single_scalar", eqm[:, :], sc[:, :],
              mx[:, TOPK - 1:TOPK], op=ALU.is_ge)
        m_negm = em.op("vector", "tensor_scalar_mul", mx[:, 16:17], mx[:, 0:1], -1.0)
        m_exp = em.op("scalar", "activation", ex[:, :], sc[:, :], AF.Exp,
                      bias=mx[:, 16:17], after=[m_negm])
        em.op("vector", "tensor_mul", ex[:, :], ex[:, :], eqm[:, :], after=[m_exp])
        em.op("vector", "reduce_sum", mx[:, 17:18], ex[:, :], axis=AX.X)
        em.op("vector", "reciprocal", mx[:, 17:18], mx[:, 17:18])
        m_g = em.op("vector", "tensor_scalar", gtok[:, :], ex[:, :],
                    mx[:, 17:18], SCALING0, op0=ALU.mult, op1=ALU.mult)
        m_tr = em.op("tensor", "transpose", pgt[:, :], gtok[:, :], ident_sb[:, :],
                     after=[m_g], inc=True)
        gt_ms[tt] = em.op("vector", "tensor_copy",
                          gt_sb[0:64, tt * 128:(tt + 1) * 128], pgt[:, :],
                          after=[m_tr])
    ge_ms = {}
    for ec in range(8):
        for qb in range(4):
            u = 0
            mm = em.op("tensor", "matmul", pge[u][:, :],
                       e64_sb[:, ec * 128:(ec + 1) * 128],
                       gt_sb[0:64, qb * 512:(qb + 1) * 512],
                       start=True, stop=True,
                       after=[m_e64, gt_ms[NTT - 1], ge_ms.get((ec * 4 + qb) - 1)],
                       inc=True)
            ge_ms[ec * 4 + qb] = em.op(
                "vector", "tensor_copy",
                gexp[:, ec * NT + qb * 512: ec * NT + qb * 512 + 512],
                pge[u][:, :], after=[mm])
    m_gexp = ge_ms[31]
    hout_ms = {}

    def evac_h(c, qb, ps, m_mm):
        lin = c * 4 + qb
        u = lin % 2
        m_ev = em.op("vector", "tensor_tensor", hstage[u][:, :], ps[:, :],
                     gexp[:, c * NT + qb * 512: c * NT + qb * 512 + 512],
                     op=ALU.mult, after=[m_mm, m_gexp, hout_ms.get(lin - 2)])
        hout_ms[lin] = em.dma("sync", out=hg_dst[c * 128:(c + 1) * 128,
                                                 qb * 512:(qb + 1) * 512],
                              in_=hstage[u][:, :], after=[m_ev])
        return m_ev

    emit_featmajor_proj(nc, ctx, em, aall, list(range(8)), xnt, evac_h, tag + "ha")
    return hout_ms[31]


def _emit_gates_l1(nc, ctx, em, gwt, ebias, la, e4, xnt, zt_dst, xn_dst,
                   ident_sb, oner_sb, tag="g1"):
    E = ctx.enter_context
    gw_sb = E(nc.sbuf_tensor(f"{tag}gw", [128, 32 * E1], BF16))
    eb_sb = E(nc.sbuf_tensor(f"{tag}eb", [1, E1], F32))
    la_sb = E(nc.sbuf_tensor(f"{tag}la", [128, 32 * 32], BF16))
    e4_sb = E(nc.sbuf_tensor(f"{tag}e4", [E1, E1 * R1], BF16))
    lg = E(nc.sbuf_tensor(f"{tag}lg", [128, E1], F32))
    mxv = E(nc.sbuf_tensor(f"{tag}mx", [128, 1], F32))
    oh = E(nc.sbuf_tensor(f"{tag}oh", [128, E1], BF16))
    oht = E(nc.sbuf_tensor(f"{tag}oht", [E1, NT], BF16))
    ztsb = E(nc.sbuf_tensor(f"{tag}zt", [32, NT], BF16))
    ztg = [E(nc.sbuf_tensor(f"{tag}ztg{i}", [32, 512], BF16)) for i in range(2)]
    plg = E(nc.psum_tensor(f"{tag}plg", [128, E1], F32))
    poh = E(nc.psum_tensor(f"{tag}poh", [E1, 128], BF16))
    pzt = E(nc.psum_tensor(f"{tag}pzt", [32, 512], F32))
    pg4 = E(nc.psum_tensor(f"{tag}pg4", [32, 512], F32))

    m_gw = em.dma("gpsimd", out=gw_sb.ap().rearrange("p (d e) -> p d e", e=E1),
                  in_=gwt.rearrange("(d p) e -> p d e", p=128))
    m_eb = em.dma("gpsimd", out=eb_sb[:, :], in_=ebias)
    m_la = em.dma("gpsimd", out=la_sb.ap().rearrange("p (d e) -> p d e", e=32),
                  in_=la.rearrange("(d p) e -> p d e", p=128))
    m_e4 = em.dma("gpsimd", out=e4_sb[:, :], in_=e4)
    for cchunk in range(32):
        em.dma("sync", out=xn_dst[cchunk * 128:(cchunk + 1) * 128, :],
               in_=xnt[:, cchunk * NT:(cchunk + 1) * NT])
    oht_ms = {}
    for tt in range(NTT):
        for d in range(32):
            em.op("tensor", "matmul", plg[:, :],
                  xnt[:, d * NT + tt * 128: d * NT + (tt + 1) * 128],
                  gw_sb[:, d * E1:(d + 1) * E1],
                  start=(d == 0), stop=False,
                  after=([m_gw, m_eb, oht_ms.get(tt - 1)] if d == 0 else ()))
        mm = em.op("tensor", "matmul", plg[:, :], oner_sb[:, :], eb_sb[:, :],
                   start=False, stop=True, inc=True)
        m_lg = em.op("vector", "tensor_copy", lg[:, :], plg[:, :], after=[mm])
        em.op("vector", "reduce_max", mxv[:, :], lg[:, :], axis=AX.X)
        m_oh = em.op("vector", "tensor_single_scalar", oh[:, :], lg[:, :],
                     mxv[:, :], op=ALU.is_equal)
        m_tr = em.op("tensor", "transpose", poh[:, :], oh[:, :], ident_sb[:, :],
                     after=[m_oh, oht_ms.get(tt - 1)], inc=True)
        oht_ms[tt] = em.op("vector", "tensor_copy",
                           oht[0:E1, tt * 128:(tt + 1) * 128], poh[:, :],
                           after=[m_tr])
    zt_ms = {}
    ztg_ms = {}
    for qb in range(4):
        u = qb % 2
        for d in range(32):
            mm = em.op("tensor", "matmul", pzt[:, :],
                       la_sb[:, d * 32:(d + 1) * 32],
                       xnt[:, d * NT + qb * 512: d * NT + qb * 512 + 512],
                       start=(d == 0), stop=(d == 31),
                       after=([m_la, zt_ms.get(qb - 1)] if d == 0 else ()),
                       inc=(d == 31))
        m_cp = em.op("scalar", "copy", ztsb[0:32, qb * 512:(qb + 1) * 512],
                     pzt[:, :], after=[mm])
        mmg = em.op("tensor", "matmul", pg4[:, :], e4_sb[:, :],
                    oht[0:E1, qb * 512:(qb + 1) * 512], start=True, stop=True,
                    after=[m_e4, oht_ms[NTT - 1], ztg_ms.get(qb - 1)], inc=True)
        m_ztg = em.op("vector", "tensor_tensor", ztg[u][:, :], pg4[:, :],
                      ztsb[0:32, qb * 512:(qb + 1) * 512], op=ALU.mult,
                      after=[mmg, m_cp])
        ztg_ms[qb] = m_ztg
        zt_ms[qb] = em.dma("sync", out=zt_dst[:, qb * 512:(qb + 1) * 512],
                           in_=ztg[u][:, :], after=[m_ztg])
    return zt_ms[3]


def build_fused():
    nc = bass.Bass("TRN2", target_bir_lowering=False)
    T = lambda n, s, d, k="ExternalInput": nc.dram_tensor(n, s, d, kind=k)
    x = T("x", [NT, D], F32).ap()
    wq0 = T("wq0", [32, 128, D], F32).ap()
    wk0 = T("wk0", [8, 128, D], F32).ap()
    wv0 = T("wv0", [D, KVD], F32).ap()
    wo0 = T("wo0", [32, 128, D], F32).ap()
    wrt0 = T("wrt0", [D, KEXP], F32).ap()
    aall0 = T("aall0", [8, 128, D], F32).ap()
    ball0 = T("ball0", [32, 128, KEXP * RANK], F32).ap()
    wq1 = T("wq1", [32, 128, D], F32).ap()
    wk1 = T("wk1", [8, 128, D], F32).ap()
    wv1 = T("wv1", [D, KVD], F32).ap()
    wo1 = T("wo1", [32, 128, D], F32).ap()
    wup1 = T("wup1", [32, 128, D], F32).ap()
    gwt1 = T("gwt1", [D, E1], F32).ap()
    ebias1 = T("ebias1", [1, E1], F32).ap()
    la1 = T("la1", [D, E1 * R1], F32).ap()
    lb1 = T("lb1", [32, 32, 128], F32).ap()
    bo1 = T("bo1", [128, 32], F32).ap()
    cost = T("cost", [128, NT], F32).ap()
    sint = T("sint", [128, NT], F32).ap()
    ident = T("ident", [128, 128], BF16).ap()
    rtm = T("rtm", [128, 128], BF16).ap()
    e64 = T("e64", [64, 1024], BF16).ap()
    e4 = T("e4", [E1, E1 * R1], BF16).ap()
    mall = T("mall", [8, 128, 512], BF16).ap()
    onec = T("onec", [128, 1], F32).ap()
    oner = T("oner", [1, 128], F32).ap()
    out = T("out", [NT, D], F32, "ExternalOutput").ap()

    q_sp = [nc.dram_tensor(f"q_sp{l}", [D, NT], BF16).ap() for l in range(2)]
    kblob = [nc.dram_tensor(f"kblob{l}", [KVD, NT], BF16) for l in range(2)]
    vblob = [nc.dram_tensor(f"vblob{l}", [NT, KVD], BF16) for l in range(2)]
    agk = [nc.dram_tensor(f"agk{l}", [2 * KVD, NT], BF16) for l in range(2)]
    agv = [nc.dram_tensor(f"agv{l}", [2 * NT, KVD], BF16) for l in range(2)]
    hg_t = nc.dram_tensor("hg_t", [KEXP * RANK, NT], BF16).ap()
    xn1_t = nc.dram_tensor("xn1_t", [D, NT], BF16).ap()
    zt_t = nc.dram_tensor("zt_t", [E1 * R1, NT], BF16).ap()
    x1 = nc.dram_tensor("x1", [NT, D], F32).ap()
    s_tok = [nc.dram_tensor(f"s_tok{i}", [NT, D], BF16).ap() for i in range(2)]

    with ExitStack() as ctx:
        E = ctx.enter_context
        sems = {n: E(nc.semaphore(n)) for n in
                ("s_sync", "s_gpsimd", "s_scalar", "s_vector", "s_tensor",
                 "d_sync", "d_gpsimd", "d_scalar", "c_cc")}
        ident_sb = E(nc.sbuf_tensor("ident_sb", [128, 128], BF16))
        mall_sb = E(nc.sbuf_tensor("mall_sb", [128, 8 * 512], BF16))
        onec_sb = E(nc.sbuf_tensor("onec_sb", [128, 1], F32))
        oner_sb = E(nc.sbuf_tensor("oner_sb", [1, 128], F32))
        shift_sb = E(nc.sbuf_tensor("shift_sb", [128, 1], F32))
        cnt = None
        for layer in range(2):
            x_in = x if layer == 0 else x1
            x_out = x1 if layer == 0 else out
            with ExitStack() as cf:
                xnt = cf.enter_context(
                    nc.sbuf_tensor(f"xnt{layer}", [128, 32 * NT], BF16))
                if layer == 0:
                    rope_stack = ExitStack()
                    rt_sb = rope_stack.enter_context(
                        nc.sbuf_tensor("rt_sb", [128, 128], BF16))
                    cos_sb = rope_stack.enter_context(
                        nc.sbuf_tensor("cos_sb", [128, NT], F32))
                    sin_sb = rope_stack.enter_context(
                        nc.sbuf_tensor("sin_sb", [128, NT], F32))
                else:
                    rt_sb = cos_sb = sin_sb = None
                with ExitStack() as c1ln:
                    em = Em(carry=cnt)
                    if layer == 0:
                        em.dma("gpsimd", out=ident_sb[:, :], in_=ident)
                        em.dma("gpsimd", out=rt_sb[:, :], in_=rtm)
                        em.dma("gpsimd", out=cos_sb[:, :], in_=cost)
                        em.dma("gpsimd", out=sin_sb[:, :], in_=sint)
                        em.dma("gpsimd",
                               out=mall_sb.ap().rearrange("p (d j) -> p d j", j=512),
                               in_=mall.rearrange("d p j -> p d j"))
                        em.dma("gpsimd", out=onec_sb[:, :], in_=onec)
                        m_cst = em.dma("gpsimd", out=oner_sb[:, :], in_=oner)
                        em.op("vector", "memset", shift_sb[:, :], ATT_SHIFT)
                        em.wait("tensor", m_cst)
                    emit_ln_to_xnt(nc, c1ln, em, x_in, xnt, ident_sb, tag=f"ln{layer}")
                    em.playback(nc, c1ln, sems)
                    cnt = em.cnt

                with ExitStack() as c2a:
                    em = Em(carry=cnt)
                    wqL, wkL, wvL = (wq0, wk0, wv0) if layer == 0 else (wq1, wk1, wv1)
                    _emit_qkv_rope(nc, c2a, em, wqL, wkL, xnt, q_sp[layer],
                                   kblob[layer].ap(), cos_sb, sin_sb, rt_sb,
                                   layer == 0, f"p{layer}")
                    em.wait("sync", ("d_sync", em.cnt["d_sync"]))
                    em.playback(nc, c2a, sems)
                    cnt = em.cnt
                if layer == 0:
                    rope_stack.close()
                with ExitStack() as c2b:
                    em = Em(carry=cnt)
                    m_agk = em.op("gpsimd", "collective_compute",
                                  "AllGather", ALU.bypass, sem_override="c_cc",
                                  replica_groups=[[2 * b, 2 * b + 1] for b in range(4)],
                                  ins=[kblob[layer].ap().opt()],
                                  outs=[agk[layer].ap().opt()])
                    _emit_vproj(nc, c2b, em, wvL, xnt, vblob[layer].ap(), f"p{layer}")
                    em.wait("sync", ("d_sync", em.cnt["d_sync"]))
                    em.playback(nc, c2b, sems)
                    cnt = em.cnt
                with ExitStack() as c2c:
                    em = Em(carry=cnt)
                    m_agv = em.op("gpsimd", "collective_compute",
                                  "AllGather", ALU.bypass, sem_override="c_cc",
                                  replica_groups=[[2 * b, 2 * b + 1] for b in range(4)],
                                  ins=[vblob[layer].ap().opt()],
                                  outs=[agv[layer].ap().opt()])
                    if layer == 0:
                        _emit_gates_l0(nc, c2c, em, wrt0, e64, xnt, hg_t, aall0,
                                       ident_sb, tag="g0")
                    else:
                        _emit_gates_l1(nc, c2c, em, gwt1, ebias1, la1, e4, xnt,
                                       zt_t, xn1_t, ident_sb, oner_sb, tag="g1")
                    em.wait("sync", ("d_sync", em.cnt["d_sync"]))
                    em.playback(nc, c2c, sems)
                    cnt = em.cnt

            with ExitStack() as ca:
                ctx_sb = ca.enter_context(
                    nc.sbuf_tensor(f"ctxsb{layer}", [128, 32 * NT], BF16))
                with ExitStack() as ca2:
                    em = Em(carry=cnt)
                    em.wait("sync", m_agk)
                    em.wait("gpsimd", m_agv)
                    _emit_attn_fused(nc, ca2, em, layer == 0, q_sp[layer],
                                     agk[layer].ap(), agv[layer].ap(), ctx_sb,
                                     ident_sb, mall_sb, onec_sb, oner_sb,
                                     shift_sb, tag=f"at{layer}")
                    em.playback(nc, ca2, sems)
                    cnt = em.cnt

                with ExitStack() as cb:
                    em = Em(carry=cnt)
                    if layer == 0:
                        hg_sb = cb.enter_context(
                            nc.sbuf_tensor("hg_sb", [128, 8 * NT], BF16))
                        m1 = em.dma("gpsimd",
                                    out=hg_sb.ap().rearrange("p (d n) -> p d n", n=NT),
                                    in_=hg_t.rearrange("(d p) n -> p d n", p=128))
                        em.wait("tensor", m1)
                        emit_back_sum(nc, cb, em,
                                      [(wo0, 32, ctx_sb), (ball0, 8, hg_sb)],
                                      s_tok[0], ident_sb, tag="b0")
                    else:
                        bo_sb = cb.enter_context(
                            nc.sbuf_tensor("bo_sb", [128, 32], F32))
                        m1 = em.dma("gpsimd", out=bo_sb[:, :], in_=bo1)
                        em.wait("scalar", m1)
                        emit_back_sum(nc, cb, em, [(wo1, 32, ctx_sb)],
                                      s_tok[0], ident_sb, bo_sb=bo_sb, tag="b1")
                    em.wait("sync", ("d_sync", em.cnt["d_sync"]))
                    em.wait("gpsimd", ("d_gpsimd", em.cnt["d_gpsimd"]))
                    em.playback(nc, cb, sems)
                    cnt = em.cnt

            if layer == 1:
                with ExitStack() as cb2:
                    big2 = cb2.enter_context(
                        nc.sbuf_tensor("big2", [128, 32 * NT], BF16))
                    ztsb2 = cb2.enter_context(nc.sbuf_tensor("ztsb2", [32, NT], BF16))
                    em = Em(carry=cnt)
                    m1 = em.dma("gpsimd",
                                out=big2.ap().rearrange("p (d n) -> p d n", n=NT),
                                in_=xn1_t.rearrange("(d p) n -> p d n", p=128))
                    m2 = em.dma("gpsimd", out=ztsb2[:, :], in_=zt_t)
                    em.wait("tensor", m1)
                    em.wait("tensor", m2)
                    emit_back_sum(nc, cb2, em,
                                  [(wup1, 32, big2), (lb1, "z32", ztsb2)],
                                  s_tok[1], ident_sb, tag="b2")
                    em.wait("sync", ("d_sync", em.cnt["d_sync"]))
                    em.wait("gpsimd", ("d_gpsimd", em.cnt["d_gpsimd"]))
                    em.playback(nc, cb2, sems)
                    cnt = em.cnt

            with ExitStack() as cr:
                em = Em(carry=cnt)
                parts = [x_in, s_tok[0]] if layer == 0 else [x_in, s_tok[0], s_tok[1]]
                emit_residual(nc, cr, em, x_out, parts, tag=f"r{layer}")
                em.wait("sync", ("d_sync", em.cnt["d_sync"]))
                em.playback(nc, cr, sems)
                cnt = em.cnt
    return nc


_FUSED_CACHE = {}


def kernel(**inputs):
    ii = _prep_host(inputs)
    tiles = [stripe_tiles(0), stripe_tiles(1)]
    malls = [_mask_table(ii, 0), _mask_table(ii, 1)]
    xg = np.ascontiguousarray(np.asarray(inputs["L0_attn_x"], np.float32)) \
        .reshape(B, 32, 128, D)
    in_maps = []
    for c in range(NCORE):
        b, half = c // 2, c % 2
        xc = np.ascontiguousarray(xg[b][tiles[half]].reshape(NT, D))
        pos = np.concatenate([np.arange(t * 128, (t + 1) * 128) for t in tiles[half]])
        in_maps.append(dict(
            x=xc,
            wq0=ii["wq0"], wk0=ii["wk0"], wv0=ii["wv0"], wo0=ii["wo0"],
            wrt0=ii["wrt0"], aall0=ii["aall0"], ball0=ii["ball0"],
            wq1=ii["wq1"], wk1=ii["wk1"], wv1=ii["wv1"], wo1=ii["wo1"],
            wup1=ii["wup1"], gwt1=ii["gwt1"], ebias1=ii["ebias1"],
            la1=ii["la1p"], lb1=ii["lb1"], bo1=ii["bo1"],
            cost=np.ascontiguousarray(ii["cosT"][:, pos]),
            sint=np.ascontiguousarray(ii["sinT"][:, pos]),
            ident=ii["ident"], rtm=ii["rT"], e64=ii["e64"], e4=ii["e4"],
            mall=malls[half],
            onec=ii["ones_col"], oner=ii["ones_row"],
        ))
    if "nc" not in _FUSED_CACHE:
        _FUSED_CACHE["nc"] = build_fused()
    nc = _FUSED_CACHE["nc"]
    res = run_bass_kernel_spmd(nc, in_maps, core_ids=list(range(NCORE)))
    outg = np.zeros((B, 32, 128, D), np.float32)
    for c in range(NCORE):
        b, half = c // 2, c % 2
        outg[b][tiles[half]] = res.results[c]["out"].reshape(16, 128, D)
    return outg.reshape(B, S, D)



def _prep_host(inputs):
    ii = {}
    ii["x"] = np.ascontiguousarray(inputs["L0_attn_x"].reshape(N, D))
    ii["wq0"] = _chunk_lhsT(np.asarray(inputs["L0_attn_Wq"], np.float32))
    ii["wk0"] = _chunk_lhsT(np.asarray(inputs["L0_attn_Wk"], np.float32))
    ii["wv0"] = np.ascontiguousarray(np.asarray(inputs["L0_attn_Wv"], np.float32))
    ii["wo0"] = _chunk_lhsT(np.asarray(inputs["L0_attn_Wo"], np.float32))
    ii["wrt0"] = np.ascontiguousarray(np.asarray(inputs["L0_ffn_Wr"], np.float32).T)
    ii["aall0"] = _chunk_lhsT(np.ascontiguousarray(
        np.asarray(inputs["L0_ffn_A"], np.float32).reshape(KEXP * RANK, D).T))
    ii["ball0"] = _chunk_lhsT_k(np.ascontiguousarray(
        np.asarray(inputs["L0_ffn_B"], np.float32).transpose(0, 2, 1).reshape(KEXP * RANK, D)))
    ii["wq1"] = _chunk_lhsT(np.ascontiguousarray(np.asarray(inputs["L1_attn_Wq"], np.float32).T))
    ii["wk1"] = _chunk_lhsT(np.ascontiguousarray(np.asarray(inputs["L1_attn_Wk"], np.float32).T))
    ii["wv1"] = np.ascontiguousarray(np.asarray(inputs["L1_attn_Wv"], np.float32).T)
    ii["wo1"] = _chunk_lhsT(np.ascontiguousarray(np.asarray(inputs["L1_attn_Wo"], np.float32).T))
    ii["wup1"] = _chunk_lhsT(np.ascontiguousarray(np.asarray(inputs["L1_ffn_W_up"], np.float32).T))
    ii["gwt1"] = np.ascontiguousarray(np.asarray(inputs["L1_ffn_gate_w"], np.float32).T)
    ii["ebias1"] = np.ascontiguousarray(np.asarray(inputs["L1_ffn_expert_bias"], np.float32).reshape(1, E1))
    ii["la1p"] = np.ascontiguousarray(np.asarray(inputs["L1_ffn_lora_A"], np.float32).reshape(E1 * R1, D).T)
    ii["lb1"] = _chunk_lhsT_k(np.ascontiguousarray(
        np.asarray(inputs["L1_ffn_lora_B"], np.float32).transpose(0, 2, 1).reshape(E1 * R1, D)))
    ii["bo1"] = np.ascontiguousarray(np.asarray(inputs["L1_attn_bo"], np.float32).reshape(32, 128).T)
    ii.update(_consts())
    return ii


def _chunk_lhsT_k(w):
    k, m = w.shape
    if k % 128 == 0:
        return _chunk_lhsT(w)
    return np.ascontiguousarray(w.reshape(k, m // 128, 128).transpose(1, 0, 2))


def run_A(ii):
    nc = build_A(0)
    in_maps = []
    for c in range(NCORE):
        half = c % 2
        in_maps.append(dict(
            x=ii["x"][c * NT:(c + 1) * NT],
            wq=ii["wq0"], wk=ii["wk0"], wv=ii["wv0"], wrt=ii["wrt0"],
            aall=ii["aall0"],
            cost=ii["cosT"][:, half * NT:(half + 1) * NT],
            sint=ii["sinT"][:, half * NT:(half + 1) * NT],
            ident=ii["ident"], rtm=ii["rT"], e64=ii["e64"],
        ))
    res = run_bass_kernel_spmd(nc, in_maps, core_ids=list(range(NCORE)))
    return res.results


def run_C(ii, ra, ctxT):
    nc = build_C()
    in_maps = []
    for c in range(NCORE):
        b, half = c // 2, c % 2
        in_maps.append(dict(
            x=ii["x"][c * NT:(c + 1) * NT],
            ctxt=np.ascontiguousarray(ctxT[b][:, half * NT:(half + 1) * NT]),
            hgt=ra[c]["hg_t"],
            wo=ii["wo0"], ball=ii["ball0"], ident=ii["ident"],
        ))
    res = run_bass_kernel_spmd(nc, in_maps, core_ids=list(range(NCORE)))
    return np.concatenate([res.results[c]["x1"] for c in range(NCORE)], axis=0)


def run_F(ii, x1, rd, ctx1T):
    nc = build_F()
    in_maps = []
    for c in range(NCORE):
        b, half = c // 2, c % 2
        in_maps.append(dict(
            x1=x1[c * NT:(c + 1) * NT],
            ctxt=np.ascontiguousarray(ctx1T[b][:, half * NT:(half + 1) * NT]),
            xnt_in=rd[c]["xn_t"], ztt=rd[c]["zt_t"],
            wo=ii["wo1"], wup=ii["wup1"], lb=ii["lb1"],
            bo=ii["bo1"], ident=ii["ident"],
        ))
    res = run_bass_kernel_spmd(nc, in_maps, core_ids=list(range(NCORE)))
    return np.concatenate([res.results[c]["out"] for c in range(NCORE)], axis=0)


def _assemble_bgh(rr):
    qT, kT, vt = [], [], []
    for b in range(B):
        qT.append(np.concatenate([rr[2 * b]["q_t"], rr[2 * b + 1]["q_t"]], axis=1))
        kT.append(np.concatenate([rr[2 * b]["k_t"], rr[2 * b + 1]["k_t"]], axis=1))
        vt.append(np.concatenate([rr[2 * b]["v_tok"], rr[2 * b + 1]["v_tok"]], axis=0))
    return qT, kT, vt


def kernel_multilaunch(**inputs):
    ii = _prep_host(inputs)
    ra = run_A(ii)
    qT, kT, vt = _assemble_bgh(ra)
    ctxT = run_attn(ii, 0, qT, kT, vt)
    x1 = run_C(ii, ra, ctxT)
    rd = run_D(ii, x1)
    q1, k1, v1 = _assemble_bgh(rd)
    ctx1T = run_attn(ii, 1, q1, k1, v1)
    out = run_F(ii, x1, rd, ctx1T)
    return out.reshape(B, S, D)


# version 07
# speedup vs baseline: 1.1560x; 1.1560x over previous
import math
from contextlib import ExitStack

import numpy as np
import ml_dtypes

import concourse.bass as bass
import concourse.mybir as mybir
from concourse.bass_utils import run_bass_kernel_spmd

F32 = mybir.dt.float32
BF16 = mybir.dt.bfloat16
AF = mybir.ActivationFunctionType
ALU = mybir.AluOpType
AX = mybir.AxisListType
BF = ml_dtypes.bfloat16

B, S, D = 4, 4096, 4096
N = B * S
NCORE = 8
NT = N // NCORE
HD = 128
NQH, NKV, GSZ = 32, 8, 4
KVD = NKV * HD
KEXP, RANK, TOPK = 64, 16, 16
SCALING0 = 32.0 / 16.0
E1, R1 = 4, 8
NTT = NT // 128
ATT_SCALE = 1.0 / math.sqrt(HD)
ATT_SHIFT = -20.0
EPS = 1e-5

ENGS = ("sync", "gpsimd", "scalar", "vector", "tensor")
CHAINED = {"scalar": True, "vector": True, "gpsimd": True, "tensor": False, "sync": False}



class Em:

    def __init__(self, carry=None):
        self.q = {e: [] for e in ENGS}
        self.cnt = dict(carry) if carry else {}
        self.sem_names = set()

    def _bump(self, sem, k):
        v = self.cnt.get(sem, 0) + k
        self.cnt[sem] = v
        self.sem_names.add(sem)
        return v

    def wait(self, eng, milestone):
        sem, val = milestone
        if val > 0:
            self.q[eng].append(("wait", sem, val))

    def op(self, eng, meth, *a, after=(), inc=False, sem_override=None, **kw):
        for m in after:
            if m is not None:
                self.wait(eng, m)
        s = "s_" + eng
        ms = None
        incs = []
        if sem_override is not None:
            incs.append((sem_override, 1))
            ms = (sem_override, self._bump(sem_override, 1))
        elif CHAINED[eng]:
            n = self.cnt.get(s, 0)
            if n > 0:
                self.q[eng].append(("wait", s, n))
            incs.append((s, 1))
            ms = (s, self._bump(s, 1))
        elif inc:
            incs.append((s, 1))
            ms = (s, self._bump(s, 1))
        self.q[eng].append(("op", meth, a, kw, incs))
        return ms

    def dma(self, eng, out, in_, after=(), **kw):
        for m in after:
            if m is not None:
                self.wait(eng, m)
        d = "d_" + eng
        ms = (d, self._bump(d, 16))
        self.q[eng].append(("op", "dma_start", (), dict(out=out, in_=in_, **kw),
                            [(d, 16)]))
        return ms

    def raw(self, eng, meth, *a, **kw):
        self.q[eng].append(("op", meth, a, kw, []))

    def playback(self, nc, ctx, sems):
        block = ctx.enter_context(nc.Block())
        for eng in ENGS:
            items = self.q[eng]
            if not items:
                continue

            def body(e, items=items):
                for it in items:
                    if it[0] == "wait":
                        e.wait_ge(sems[it[1]], it[2])
                    else:
                        _, meth, a, kw, incs = it
                        inst = getattr(e, meth)(*a, **kw)
                        for (sname, k) in incs:
                            inst.then_inc(sems[sname], k)
            getattr(block, eng)(body)
        return sems



def _chunk_lhsT(w):
    k, m = w.shape
    return np.ascontiguousarray(
        w.reshape(k // 128, 128, m // 128, 128).transpose(2, 1, 0, 3)
        .reshape(m // 128, 128, k))


def _consts():
    ident = np.eye(128, dtype=BF)
    Rm = np.zeros((128, 128), np.float32)
    for i in range(64):
        Rm[i, i + 64] = -1.0
        Rm[i + 64, i] = 1.0
    rT = np.ascontiguousarray(Rm.T).astype(BF)
    inv_freq = 1.0 / (10000.0 ** (np.arange(0, HD, 2, dtype=np.float64) / HD))
    ang = np.arange(S, dtype=np.float64)[:, None] * inv_freq[None, :]
    ang2 = np.concatenate([ang, ang], axis=-1)
    cosT = np.ascontiguousarray(np.cos(ang2).T).astype(np.float32)
    sinT = np.ascontiguousarray(np.sin(ang2).T).astype(np.float32)
    mv = np.zeros((4, 128, 512), np.float32)
    for d in range(4):
        for j in range(4):
            if j < d:
                mv[d, :, j * 128:(j + 1) * 128] = -1e9
            elif j == d:
                blk = np.where(np.arange(128)[:, None] > np.arange(128)[None, :],
                               np.float32(-1e9), np.float32(0.0))
                mv[d, :, j * 128:(j + 1) * 128] = blk
    maskvar = np.ascontiguousarray(mv.astype(BF))
    e64 = np.zeros((64, 1024), np.float32)
    for e in range(64):
        e64[e, e * 16:(e + 1) * 16] = 1.0
    e4 = np.zeros((4, 32), np.float32)
    for e in range(4):
        e4[e, e * 8:(e + 1) * 8] = 1.0
    ones_col = np.ones((128, 1), np.float32)
    ones_row = np.ones((1, 128), np.float32)
    return dict(ident=ident, rT=rT, cosT=cosT, sinT=sinT, maskvar=maskvar,
                e64=np.ascontiguousarray(e64.astype(BF)),
                e4=np.ascontiguousarray(e4.astype(BF)),
                ones_col=ones_col, ones_row=ones_row)



def emit_ln_to_xnt(nc, ctx, em, x_ap, xnt, ident_sb, tag="ln"):
    E = ctx.enter_context
    xt = [E(nc.sbuf_tensor(f"{tag}xt{i}", [128, D], F32)) for i in range(2)]
    xn = [E(nc.sbuf_tensor(f"{tag}xn{i}", [128, D], BF16)) for i in range(2)]
    st = E(nc.sbuf_tensor(f"{tag}st", [128, 8], F32))
    tp = [E(nc.psum_tensor(f"{tag}tp{i}", [128, 128], BF16)) for i in range(4)]

    ev_ms = {}
    ap_ms = {}
    sq_ms = {}
    tr_ms = {}
    for tt in range(NTT):
        dep = [ap_ms.get(tt - 2), sq_ms.get(tt - 2)]
        m_dma = em.dma("sync", out=xt[tt % 2][:, :],
                       in_=x_ap[tt * 128:(tt + 1) * 128, :], after=dep)
        m_sq = em.op("scalar", "activation", xn[tt % 2][:, :], xt[tt % 2][:, :],
                     AF.Square, accum_out=st[:, 1:2],
                     after=[m_dma, tr_ms.get((tt - 2, 31))])
        sq_ms[tt] = m_sq
        em.op("vector", "reduce_sum", st[:, 0:1], xt[tt % 2][:, :], axis=AX.X,
              after=[m_dma])
        em.op("vector", "tensor_scalar_mul", st[:, 2:3], st[:, 0:1], 1.0 / D)
        em.op("vector", "tensor_scalar_mul", st[:, 3:4], st[:, 1:2], 1.0 / D,
              after=[m_sq])
        em.op("vector", "tensor_mul", st[:, 7:8], st[:, 2:3], st[:, 2:3])
        em.op("vector", "tensor_sub", st[:, 4:5], st[:, 3:4], st[:, 7:8])
        m_var = em.op("vector", "tensor_scalar_add", st[:, 4:5], st[:, 4:5], EPS)
        em.op("scalar", "activation", st[:, 5:6], st[:, 4:5], AF.Ln, after=[m_var])
        m_rstd = em.op("scalar", "activation", st[:, 6:7], st[:, 5:6], AF.Exp,
                       scale=-0.5)
        ap_ms[tt] = em.op("vector", "tensor_scalar", xn[tt % 2][:, :],
                          xt[tt % 2][:, :], st[:, 2:3], st[:, 6:7],
                          op0=ALU.subtract, op1=ALU.mult,
                          after=[m_rstd, tr_ms.get((tt - 2, 31))])
        for c in range(32):
            prev = (tt, c - 4) if c >= 4 else (tt - 1, c + 28)
            m_tr = em.op("tensor", "transpose", tp[c % 4][:, :],
                         xn[tt % 2][:, c * 128:(c + 1) * 128], ident_sb[:, :],
                         after=[ap_ms[tt], ev_ms.get(prev)], inc=True)
            tr_ms[(tt, c)] = m_tr
            ev_ms[(tt, c)] = em.op("vector", "tensor_copy",
                                   xnt[:, c * NT + tt * 128: c * NT + (tt + 1) * 128],
                                   tp[c % 4][:, :], after=[m_tr])
    return ev_ms[(NTT - 1, 31)]


def emit_featmajor_proj(nc, ctx, em, w_ap, chunks, xnt, evac_cb, tag,
                        kdim=D, ntok=NT, ready=None, dma_eng="gpsimd", pj=None):
    E = ctx.enter_context
    nk = kdim // 128
    nqb = ntok // 512
    slab = [E(nc.sbuf_tensor(f"{tag}slab{i}", [128, nk * 128], BF16)) for i in range(2)]
    if pj is None:
        pj = [E(nc.psum_tensor(f"{tag}pj{i}", [128, 512], F32)) for i in range(2)]
    mm_ms = {}
    free_ms = {}
    dw_ms = {}
    for ci, c in enumerate(chunks):
        dep = [mm_ms.get((ci - 2, nqb - 1)), ready if ci == 0 else None]
        dw_ms[ci] = em.dma(dma_eng, out=slab[ci % 2][:, :], in_=w_ap[c],
                           after=dep)
        for qb in range(nqb):
            u = (ci * nqb + qb) % 2
            pu = ci * nqb + qb - 2
            deps = [dw_ms[ci] if qb == 0 else None,
                    free_ms.get(divmod(pu, nqb)) if pu >= 0 else None]
            if ci == 0 and qb == 0 and ready is not None:
                deps.append(ready)
            for d in range(nk):
                last = d == nk - 1
                mm = em.op("tensor", "matmul", pj[u][:, :],
                           slab[ci % 2][:, d * 128:(d + 1) * 128],
                           xnt[:, d * ntok + qb * 512: d * ntok + qb * 512 + 512],
                           start=(d == 0), stop=last,
                           after=deps if d == 0 else (), inc=last)
            mm_ms[(ci, qb)] = mm
            free_ms[(ci, qb)] = evac_cb(c, qb, pj[u], mm)
    return mm_ms, free_ms



def build_A(core):
    nc = bass.Bass("TRN2", target_bir_lowering=False)
    ap = lambda t: t.ap()
    x = nc.dram_tensor("x", [NT, D], F32, kind="ExternalInput").ap()
    wq = nc.dram_tensor("wq", [32, 128, D], F32, kind="ExternalInput").ap()
    wk = nc.dram_tensor("wk", [8, 128, D], F32, kind="ExternalInput").ap()
    wv = nc.dram_tensor("wv", [D, KVD], F32, kind="ExternalInput").ap()
    wrt = nc.dram_tensor("wrt", [D, KEXP], F32, kind="ExternalInput").ap()
    aall = nc.dram_tensor("aall", [8, 128, D], F32, kind="ExternalInput").ap()
    cost = nc.dram_tensor("cost", [128, NT], F32, kind="ExternalInput").ap()
    sint = nc.dram_tensor("sint", [128, NT], F32, kind="ExternalInput").ap()
    ident = nc.dram_tensor("ident", [128, 128], BF16, kind="ExternalInput").ap()
    rtm = nc.dram_tensor("rtm", [128, 128], BF16, kind="ExternalInput").ap()
    e64 = nc.dram_tensor("e64", [64, 1024], BF16, kind="ExternalInput").ap()

    q_t = nc.dram_tensor("q_t", [D, NT], BF16, kind="ExternalOutput").ap()
    k_t = nc.dram_tensor("k_t", [KVD, NT], BF16, kind="ExternalOutput").ap()
    v_tok = nc.dram_tensor("v_tok", [NT, KVD], BF16, kind="ExternalOutput").ap()
    hg_t = nc.dram_tensor("hg_t", [KEXP * RANK, NT], BF16, kind="ExternalOutput").ap()

    with ExitStack() as ctx:
        E = ctx.enter_context
        sems = {n: E(nc.semaphore(n)) for n in
                ("s_sync", "s_gpsimd", "s_scalar", "s_vector", "s_tensor",
                 "d_sync", "d_gpsimd", "d_scalar")}
        xnt = E(nc.sbuf_tensor("xnt", [128, 32 * NT], BF16))
        ident_sb = E(nc.sbuf_tensor("ident_sb", [128, 128], BF16))
        rt_sb = E(nc.sbuf_tensor("rt_sb", [128, 128], BF16))
        cos_sb = E(nc.sbuf_tensor("cos_sb", [128, NT], F32))
        sin_sb = E(nc.sbuf_tensor("sin_sb", [128, NT], F32))

        with ExitStack() as c1:
            em = Em()
            em.dma("gpsimd", out=ident_sb[:, :], in_=ident)
            em.dma("gpsimd", out=cos_sb[:, :], in_=cost)
            em.dma("gpsimd", out=sin_sb[:, :], in_=sint)
            m_cst = em.dma("gpsimd", out=rt_sb[:, :], in_=rtm)
            em.wait("tensor", m_cst)
            emit_ln_to_xnt(nc, c1, em, x, xnt, ident_sb)
            em.playback(nc, c1, sems)

        with ExitStack() as c2:
            E2 = c2.enter_context
            em = Em(carry=em.cnt)
            stage = [E2(nc.sbuf_tensor(f"qst{i}", [128, NT], BF16)) for i in range(2)]
            qraw = [E2(nc.sbuf_tensor(f"qraw{i}", [128, 512], BF16)) for i in range(2)]
            t1 = [E2(nc.sbuf_tensor(f"ropet1_{i}", [128, 512], F32)) for i in range(2)]
            rot = [E2(nc.psum_tensor(f"rope_rot{i}", [128, 512], F32)) for i in range(2)]
            st_out_ms = {}
            rope_add = {}
            rope_mm = {}

            def evac_qk(c, qb, ps, m_mm):
                lin = c * 4 + qb
                dma_guard = st_out_ms.get(c - 2) if qb == 0 else None
                m_qraw = em.op("scalar", "copy", qraw[qb % 2][:, :], ps[:, :],
                               after=[m_mm, rope_add.get(lin - 2),
                                      rope_mm.get(lin - 2), dma_guard])
                m_rot = em.op("tensor", "matmul", rot[qb % 2][:, :], rt_sb[:, :],
                              qraw[qb % 2][:, :], start=True, stop=True,
                              after=[m_qraw, rope_add.get(lin - 2)], inc=True)
                rope_mm[lin] = m_rot
                em.op("vector", "tensor_tensor", t1[qb % 2][:, :], qraw[qb % 2][:, :],
                      cos_sb[:, qb * 512:(qb + 1) * 512], op=ALU.mult, after=[m_qraw])
                em.op("vector", "scalar_tensor_tensor",
                      stage[c % 2][:, qb * 512:(qb + 1) * 512],
                      rot[qb % 2][:, :], 1.0, sin_sb[:, qb * 512:(qb + 1) * 512],
                      op0=ALU.mult, op1=ALU.mult, after=[m_rot])
                m_add = em.op("vector", "tensor_tensor",
                              stage[c % 2][:, qb * 512:(qb + 1) * 512],
                              stage[c % 2][:, qb * 512:(qb + 1) * 512],
                              t1[qb % 2][:, :], op=ALU.add)
                rope_add[lin] = m_add
                if qb == 3:
                    dst = q_t[c * 128:(c + 1) * 128, :] if c < 32 else \
                        k_t[(c - 32) * 128:(c - 31) * 128, :]
                    st_out_ms[c] = em.dma("sync", out=dst, in_=stage[c % 2][:, :],
                                          after=[m_add])
                return m_qraw

            emit_featmajor_proj(nc, c2, em, wq, list(range(32)), xnt, evac_qk, "wq")

            def evac_k(c, qb, ps, m_mm):
                return evac_qk(c + 32, qb, ps, m_mm)
            emit_featmajor_proj(nc, c2, em, wk, list(range(8)), xnt, evac_k, "wk")
            em.playback(nc, c2, sems)

        with ExitStack() as c3:
            E3 = c3.enter_context
            em = Em(carry=em.cnt)
            wv_sb = E3(nc.sbuf_tensor("wv_sb", [128, 32 * 512], BF16))
            vstage = [E3(nc.sbuf_tensor(f"vst{i}", [128, 512], BF16)) for i in range(2)]
            pv = [E3(nc.psum_tensor(f"pv{i}", [128, 512], F32)) for i in range(2)]
            last_mm = None
            for half in range(2):
                m_w = em.dma("gpsimd",
                             out=wv_sb.ap().rearrange("p (d j) -> p d j", j=512),
                             in_=wv[:, half * 512:(half + 1) * 512]
                             .rearrange("(d p) j -> p d j", p=128),
                             after=[last_mm])
                ev = {}
                for tt in range(NTT):
                    u = tt % 2
                    for d in range(32):
                        mm = em.op("tensor", "matmul", pv[u][:, :],
                                   xnt[:, d * NT + tt * 128: d * NT + (tt + 1) * 128],
                                   wv_sb[:, d * 512:(d + 1) * 512],
                                   start=(d == 0), stop=(d == 31),
                                   after=([m_w, ev.get(tt - 2)] if d == 0 else ()),
                                   inc=(d == 31))
                    last_mm = mm
                    m_cp = em.op("scalar", "copy", vstage[u][:, :], pv[u][:, :],
                                 after=[mm])
                    m_out = em.dma("sync", out=v_tok[tt * 128:(tt + 1) * 128,
                                                    half * 512:(half + 1) * 512],
                                   in_=vstage[u][:, :], after=[m_cp])
                    ev[tt] = m_out
            em.playback(nc, c3, sems)

        with ExitStack() as c4:
            E4 = c4.enter_context
            em = Em(carry=em.cnt)
            wr_sb = E4(nc.sbuf_tensor("wr_sb", [128, 32 * 64], BF16))
            e64_sb = E4(nc.sbuf_tensor("e64_sb", [64, 1024], BF16))
            sc = E4(nc.sbuf_tensor("sc", [128, 64], F32))
            work = E4(nc.sbuf_tensor("work", [128, 64], F32))
            eqm = E4(nc.sbuf_tensor("eqm", [128, 64], F32))
            ex = E4(nc.sbuf_tensor("ex", [128, 64], F32))
            mx = E4(nc.sbuf_tensor("mx", [128, 18], F32))
            gtok = E4(nc.sbuf_tensor("gtok", [128, 64], BF16))
            gt_sb = E4(nc.sbuf_tensor("gt_sb", [64, NT], BF16))
            gexp = E4(nc.sbuf_tensor("gexp", [128, 8 * NT], BF16))
            hstage = [E4(nc.sbuf_tensor(f"hst{i}", [128, 512], BF16)) for i in range(2)]
            psc = E4(nc.psum_tensor("psc", [128, 64], F32))
            pgt = E4(nc.psum_tensor("pgt", [64, 128], BF16))
            pge = [E4(nc.psum_tensor(f"pge{i}", [128, 512], F32)) for i in range(2)]

            m_wr = em.dma("gpsimd",
                          out=wr_sb.ap().rearrange("p (d e) -> p d e", e=64),
                          in_=wrt.rearrange("(d p) e -> p d e", p=128))
            m_e64 = em.dma("gpsimd", out=e64_sb[:, :], in_=e64)

            gt_ms = {}
            for tt in range(NTT):
                for d in range(32):
                    mm = em.op("tensor", "matmul", psc[:, :],
                               xnt[:, d * NT + tt * 128: d * NT + (tt + 1) * 128],
                               wr_sb[:, d * 64:(d + 1) * 64],
                               start=(d == 0), stop=(d == 31),
                               after=([m_wr, gt_ms.get(tt - 1)] if d == 0 else ()),
                               inc=(d == 31))
                m_sc = em.op("vector", "tensor_copy", sc[:, :], psc[:, :], after=[mm])
                em.op("vector", "tensor_copy", work[:, :], sc[:, :])
                em.op("vector", "reduce_max", mx[:, 0:1], work[:, :], axis=AX.X)
                for i in range(1, TOPK):
                    em.op("vector", "tensor_single_scalar", eqm[:, :], work[:, :],
                          mx[:, i - 1:i], op=ALU.is_equal)
                    em.op("vector", "scalar_tensor_tensor", work[:, :], eqm[:, :],
                          -1e30, work[:, :], op0=ALU.mult, op1=ALU.add)
                    em.op("vector", "reduce_max", mx[:, i:i + 1], work[:, :], axis=AX.X)
                em.op("vector", "tensor_single_scalar", eqm[:, :], sc[:, :],
                      mx[:, TOPK - 1:TOPK], op=ALU.is_ge)
                m_negm = em.op("vector", "tensor_scalar_mul", mx[:, 16:17],
                               mx[:, 0:1], -1.0)
                m_exp = em.op("scalar", "activation", ex[:, :], sc[:, :], AF.Exp,
                              bias=mx[:, 16:17], after=[m_negm])
                em.op("vector", "tensor_mul", ex[:, :], ex[:, :], eqm[:, :],
                      after=[m_exp])
                em.op("vector", "reduce_sum", mx[:, 17:18], ex[:, :], axis=AX.X)
                em.op("vector", "reciprocal", mx[:, 17:18], mx[:, 17:18])
                m_g = em.op("vector", "tensor_scalar", gtok[:, :], ex[:, :],
                            mx[:, 17:18], SCALING0, op0=ALU.mult, op1=ALU.mult)
                m_tr = em.op("tensor", "transpose", pgt[:, :], gtok[:, :],
                             ident_sb[:, :], after=[m_g], inc=True)
                gt_ms[tt] = em.op("vector", "tensor_copy",
                                  gt_sb[0:64, tt * 128:(tt + 1) * 128], pgt[:, :],
                                  after=[m_tr])
            ge_ms = {}
            for ec in range(8):
                for qb in range(4):
                    u = (ec * 4 + qb) % 2
                    mm = em.op("tensor", "matmul", pge[u][:, :],
                               e64_sb[:, ec * 128:(ec + 1) * 128],
                               gt_sb[0:64, qb * 512:(qb + 1) * 512],
                               start=True, stop=True,
                               after=[m_e64, gt_ms[NTT - 1],
                                      ge_ms.get((ec * 4 + qb) - 2)], inc=True)
                    ge_ms[ec * 4 + qb] = em.op(
                        "vector", "tensor_copy",
                        gexp[:, ec * NT + qb * 512: ec * NT + qb * 512 + 512],
                        pge[u][:, :], after=[mm])
            m_gexp = ge_ms[31]

            hout_ms = {}

            def evac_h(c, qb, ps, m_mm):
                lin = c * 4 + qb
                u = lin % 2
                m_ev = em.op("vector", "tensor_tensor", hstage[u][:, :], ps[:, :],
                             gexp[:, c * NT + qb * 512: c * NT + qb * 512 + 512],
                             op=ALU.mult, after=[m_mm, m_gexp,
                                                 hout_ms.get(lin - 2)])
                hout_ms[lin] = em.dma("sync",
                                      out=hg_t[c * 128:(c + 1) * 128,
                                               qb * 512:(qb + 1) * 512],
                                      in_=hstage[u][:, :], after=[m_ev])
                return m_ev

            emit_featmajor_proj(nc, c4, em, aall, list(range(8)), xnt, evac_h, "ha")
            for dn in ("d_sync", "d_gpsimd", "d_scalar"):
                if em.cnt.get(dn, 0) > 0:
                    eng = dn[2:]
                    em.wait(eng, (dn, em.cnt[dn]))
            em.playback(nc, c4, sems)

    return nc




def build_attn(causal):
    nc = bass.Bass("TRN2", target_bir_lowering=False)
    qt_in = nc.dram_tensor("qt_in", [2048, S], BF16, kind="ExternalInput").ap()
    kt_in = nc.dram_tensor("kt_in", [512, S], BF16, kind="ExternalInput").ap()
    v_in = nc.dram_tensor("v_in", [S, 512], BF16, kind="ExternalInput").ap()
    ident = nc.dram_tensor("ident", [128, 128], BF16, kind="ExternalInput").ap()
    mv = nc.dram_tensor("mv", [4, 128, 512], BF16, kind="ExternalInput").ap()
    onec = nc.dram_tensor("onec", [128, 1], F32, kind="ExternalInput").ap()
    oner = nc.dram_tensor("oner", [1, 128], F32, kind="ExternalInput").ap()
    ctx_t = nc.dram_tensor("ctx_t", [2048, S], BF16, kind="ExternalOutput").ap()

    NKT = S // 128
    NQBJ = S // 512

    with ExitStack() as ctx:
        E = ctx.enter_context
        sems = {n: E(nc.semaphore(n)) for n in
                ("s_sync", "s_gpsimd", "s_scalar", "s_vector", "s_tensor",
                 "d_sync", "d_gpsimd", "d_scalar")}
        kg = [E(nc.sbuf_tensor(f"kg{i}", [128, S], BF16)) for i in range(2)]
        vg = [E(nc.sbuf_tensor(f"vg{i}", [128, NKT * 128], BF16)) for i in range(2)]
        qh = [E(nc.sbuf_tensor(f"qh{i}", [128, S], BF16)) for i in range(2)]
        ident_sb = E(nc.sbuf_tensor("ident_sb", [128, 128], BF16))
        mv_sb = E(nc.sbuf_tensor("mv_sb", [128, 4 * 512], BF16))
        onec_sb = E(nc.sbuf_tensor("onec_sb", [128, 1], F32))
        oner_sb = E(nc.sbuf_tensor("oner_sb", [1, 128], F32))
        shift_sb = E(nc.sbuf_tensor("shift_sb", [128, 1], F32))
        p_sb = [E(nc.sbuf_tensor(f"p{i}", [128, 512], BF16)) for i in range(2)]
        acc = E(nc.sbuf_tensor("acc", [128, 512], F32))
        rsig = E(nc.sbuf_tensor("rsig", [1, 512], F32))
        r_sb = E(nc.sbuf_tensor("r_sb", [128, 512], F32))
        cst = [E(nc.sbuf_tensor(f"cst{i}", [128, 512], BF16)) for i in range(2)]
        ps_s = [E(nc.psum_tensor(f"ps_s{i}", [128, 512], F32)) for i in range(2)]
        ps_c = E(nc.psum_tensor("ps_c", [128, 512], F32))
        ps_sig = E(nc.psum_tensor("ps_sig", [1, 512], F32))
        ps_r = E(nc.psum_tensor("ps_r", [128, 512], F32))

        em = Em()
        em.dma("gpsimd", out=ident_sb[:, :], in_=ident)
        em.dma("gpsimd", out=mv_sb.ap().rearrange("p (d j) -> p d j", j=512),
               in_=mv.rearrange("d p j -> p d j"))
        em.dma("gpsimd", out=onec_sb[:, :], in_=onec)
        m_cst = em.dma("gpsimd", out=oner_sb[:, :], in_=oner)
        m_shift = em.op("vector", "memset", shift_sb[:, :], ATT_SHIFT)

        kload = {}
        vload = {}
        qload = {}
        last_k_use = {}
        last_v_use = {}
        last_q_use = {}
        evac_ms = {}
        out_ms = {}
        sig_ms = {}
        rcp_ms = {}

        unit = 0
        for g in range(4):
            kload[g] = em.dma("sync", out=kg[g % 2][:, :],
                              in_=kt_in[g * 128:(g + 1) * 128, :],
                              after=[last_k_use.get(g - 2)])
            vload[g] = em.dma("sync",
                              out=vg[g % 2].ap().rearrange("p (t d) -> p t d", d=128),
                              in_=v_in[:, g * 128:(g + 1) * 128]
                              .rearrange("(t p) d -> p t d", p=128),
                              after=[last_v_use.get(g - 2)])
            for h in range(4):
                gh = g * 4 + h
                qload[gh] = em.dma("sync", out=qh[gh % 2][:, :],
                                   in_=qt_in[gh * 128:(gh + 1) * 128, :],
                                   after=[last_q_use.get(gh - 2)])
                for qb in range(NQBJ):
                    qt0 = qb * 4
                    kts = list(range(qt0 + 4)) if causal else list(range(NKT))
                    nk = len(kts)
                    m_ms = em.op("vector", "memset", acc[:, :], 0.0,
                                 after=[sig_ms.get(unit - 1)])
                    exp_ms = {}
                    mm_ms = {}
                    av_last = None
                    for i in range(nk + 1):
                        if i < nk:
                            kt = kts[i]
                            u = i % 2
                            deps = [exp_ms.get(i - 2)]
                            if i == 0:
                                deps += [kload[g], qload[gh], m_shift]
                            boundary = causal and kt >= qt0
                            mm = em.op("tensor", "matmul", ps_s[u][:, :],
                                       kg[g % 2][:, kt * 128:(kt + 1) * 128],
                                       qh[gh % 2][:, qb * 512:(qb + 1) * 512],
                                       start=True, stop=not boundary,
                                       after=deps, inc=not boundary)
                            if boundary:
                                d = kt - qt0
                                mm = em.op("tensor", "matmul", ps_s[u][:, :],
                                           ident_sb[:, :],
                                           mv_sb[:, d * 512:(d + 1) * 512],
                                           start=False, stop=True, inc=True)
                            mm_ms[i] = mm
                            exp_ms[i] = em.op("scalar", "activation", p_sb[u][:, :],
                                              ps_s[u][:, :], AF.Exp,
                                              bias=shift_sb[:, :], scale=ATT_SCALE,
                                              after=[mm])
                        if i >= 1:
                            j = i - 1
                            u = j % 2
                            deps = [exp_ms[j]]
                            if j == 0:
                                deps += [evac_ms.get(unit - 1), vload[g]]
                            av = em.op("tensor", "matmul", ps_c[:, :],
                                       vg[g % 2][:, kts[j] * 128:(kts[j] + 1) * 128],
                                       p_sb[u][:, :],
                                       start=(j == 0), stop=(j == nk - 1),
                                       after=deps, inc=(j == nk - 1))
                            av_last = av
                            em.op("vector", "tensor_tensor", acc[:, :], acc[:, :],
                                  p_sb[u][:, :], op=ALU.add, after=[exp_ms[j]])
                    last_k_use[g] = mm_ms[nk - 1]
                    last_q_use[gh] = mm_ms[nk - 1]
                    last_v_use[g] = av_last
                    m_sig = em.op("tensor", "matmul", ps_sig[:, :], onec_sb[:, :],
                                  acc[:, :], start=True, stop=True,
                                  after=[rcp_ms.get(unit - 1)], inc=True)
                    sig_ms[unit] = m_sig
                    m_rec = em.op("vector", "reciprocal", rsig[:, :], ps_sig[:, :],
                                  after=[m_sig])
                    m_rmm = em.op("tensor", "matmul", ps_r[:, :], oner_sb[:, :],
                                  rsig[:, :], start=True, stop=True,
                                  after=[m_rec], inc=True)
                    rcp_ms[unit] = em.op("vector", "tensor_copy", r_sb[:, :],
                                         ps_r[:, :], after=[m_rmm])
                    m_ev = em.op("vector", "tensor_tensor", cst[unit % 2][:, :],
                                 ps_c[:, :], r_sb[:, :], op=ALU.mult,
                                 after=[av_last, out_ms.get(unit - 2)])
                    evac_ms[unit] = m_ev
                    out_ms[unit] = em.dma("sync",
                                          out=ctx_t[gh * 128:(gh + 1) * 128,
                                                    qb * 512:(qb + 1) * 512],
                                          in_=cst[unit % 2][:, :], after=[m_ev])
                    unit += 1
        for dn in ("d_sync", "d_gpsimd"):
            if em.cnt.get(dn, 0) > 0:
                em.wait(dn[2:], (dn, em.cnt[dn]))
        em.playback(nc, ctx, sems)
    return nc


def run_attn(ii, layer, qT, kT, v_tok):
    nc = build_attn(causal=(layer == 0))
    in_maps = []
    for c in range(NCORE):
        b, half = c // 2, c % 2
        g0 = 4 * half
        in_maps.append(dict(
            qt_in=np.ascontiguousarray(qT[b][g0 * 512:(g0 + 4) * 512]),
            kt_in=np.ascontiguousarray(kT[b][g0 * 128:(g0 + 4) * 128]),
            v_in=np.ascontiguousarray(v_tok[b][:, g0 * 128:(g0 + 4) * 128]),
            ident=ii["ident"], mv=ii["maskvar"],
            onec=ii["ones_col"], oner=ii["ones_row"],
        ))
    res = run_bass_kernel_spmd(nc, in_maps, core_ids=list(range(NCORE)))
    ctxT = []
    for b in range(B):
        ctxT.append(np.concatenate(
            [res.results[2 * b]["ctx_t"], res.results[2 * b + 1]["ctx_t"]], axis=0))
    return np.stack(ctxT)




def emit_back_sum(nc, ctx, em, streams, out_tok, ident_sb, bo_sb=None, tag="bk"):
    E = ctx.enter_context
    slabs = []
    for si, (w_ap, nk, src_sb) in enumerate(streams):
        kparts = 32 if nk == "z32" else 128
        nkc = 1 if nk == "z32" else nk
        slabs.append([E(nc.sbuf_tensor(f"{tag}sl{si}_{i}", [kparts, nkc * 128], BF16))
                      for i in range(2)])
    pj = [E(nc.psum_tensor(f"{tag}pj{i}", [128, 512], F32)) for i in range(2)]
    tp = [E(nc.psum_tensor(f"{tag}tp{i}", [128, 128], BF16)) for i in range(2)]
    sumbf = [E(nc.sbuf_tensor(f"{tag}sb{i}", [128, 512], BF16)) for i in range(2)]
    stok = [E(nc.sbuf_tensor(f"{tag}st{i}", [128, 128], BF16)) for i in range(2)]

    dw_ms = {}
    mm_ms = {}
    ev_ms = {}
    cp_ms = {}
    do_ms = {}
    tr_ms = {}
    for c in range(32):
        deps = [mm_ms.get((c - 2, 3))]
        for si, (w_ap, nk, src_sb) in enumerate(streams):
            if nk == "z32":
                dw_ms[(c, si)] = em.dma("gpsimd", out=slabs[si][c % 2][:, :],
                                        in_=w_ap[c], after=deps)
            else:
                dw_ms[(c, si)] = em.dma("gpsimd", out=slabs[si][c % 2][:, :],
                                        in_=w_ap[c], after=deps)
        nstream = len(streams)
        for qb in range(4):
            u = (c * 4 + qb) % 2
            lin = c * 4 + qb
            first = True
            for si, (w_ap, nk, src_sb) in enumerate(streams):
                nkc = 1 if nk == "z32" else nk
                for d in range(nkc):
                    deps2 = []
                    if first:
                        deps2 = [dw_ms[(c, s)] for s in range(nstream)]
                        deps2.append(ev_ms.get(lin - 2))
                    last = (si == nstream - 1) and (d == nkc - 1)
                    if nk == "z32":
                        rhs = src_sb[0:32, qb * 512:(qb + 1) * 512]
                        lhsT = slabs[si][c % 2][:, :]
                    else:
                        rhs = src_sb[:, d * NT + qb * 512: d * NT + qb * 512 + 512]
                        lhsT = slabs[si][c % 2][:, d * 128:(d + 1) * 128]
                    mm = em.op("tensor", "matmul", pj[u][:, :], lhsT, rhs,
                               start=first, stop=last, after=deps2, inc=last)
                    first = False
            mm_ms[(c, qb)] = mm
            if bo_sb is None:
                ev_ms[lin] = em.op("scalar", "copy", sumbf[u][:, :], pj[u][:, :],
                                   after=[mm])
            else:
                ev_ms[lin] = em.op("scalar", "add", sumbf[u][:, :], pj[u][:, :],
                                   bo_sb[:, c:c + 1], after=[mm])
            for j in range(4):
                tu = (lin * 4 + j) % 2
                m_tr = em.op("tensor", "transpose", tp[tu][:, :],
                             sumbf[u][:, j * 128:(j + 1) * 128], ident_sb[:, :],
                             after=[ev_ms[lin], cp_ms.get((lin * 4 + j) - 2)],
                             inc=True)
                tr_ms[lin] = m_tr
                m_cp = em.op("vector", "tensor_copy", stok[tu][:, :], tp[tu][:, :],
                             after=[m_tr, do_ms.get((lin * 4 + j) - 2)])
                cp_ms[lin * 4 + j] = m_cp
                tt = qb * 4 + j
                do_ms[lin * 4 + j] = em.dma(
                    "sync", out=out_tok[tt * 128:(tt + 1) * 128, c * 128:(c + 1) * 128],
                    in_=stok[tu][:, :], after=[m_cp])
    return do_ms[32 * 4 * 4 - 1]


def emit_residual(nc, ctx, em, out_ap, parts, tag="res"):
    E = ctx.enter_context
    bufs = [[E(nc.sbuf_tensor(f"{tag}b{pi}_{i}", [128, D], F32 if pi == 0 else BF16))
             for i in range(2)] for pi in range(len(parts))]
    ob = [E(nc.sbuf_tensor(f"{tag}o{i}", [128, D], F32)) for i in range(2)]
    add_ms = {}
    out_ms = {}
    for tt in range(NTT):
        u = tt % 2
        lms = []
        for pi, pap in enumerate(parts):
            lms.append(em.dma("sync", out=bufs[pi][u][:, :],
                              in_=pap[tt * 128:(tt + 1) * 128, :],
                              after=[add_ms.get(tt - 2)]))
        m = em.op("vector", "tensor_tensor", ob[u][:, :], bufs[0][u][:, :],
                  bufs[1][u][:, :], op=ALU.add, after=lms + [out_ms.get(tt - 2)])
        for pi in range(2, len(parts)):
            m = em.op("vector", "tensor_tensor", ob[u][:, :], ob[u][:, :],
                      bufs[pi][u][:, :], op=ALU.add)
        add_ms[tt] = m
        out_ms[tt] = em.dma("sync", out=out_ap[tt * 128:(tt + 1) * 128, :],
                            in_=ob[u][:, :], after=[m])
    return out_ms[NTT - 1]


def build_C():
    nc = bass.Bass("TRN2", target_bir_lowering=False)
    x = nc.dram_tensor("x", [NT, D], F32, kind="ExternalInput").ap()
    ctxt = nc.dram_tensor("ctxt", [D, NT], BF16, kind="ExternalInput").ap()
    hgt = nc.dram_tensor("hgt", [KEXP * RANK, NT], BF16, kind="ExternalInput").ap()
    wo = nc.dram_tensor("wo", [32, 128, D], F32, kind="ExternalInput").ap()
    ball = nc.dram_tensor("ball", [32, 128, KEXP * RANK], F32, kind="ExternalInput").ap()
    ident = nc.dram_tensor("ident", [128, 128], BF16, kind="ExternalInput").ap()
    x1 = nc.dram_tensor("x1", [NT, D], F32, kind="ExternalOutput").ap()
    sum_tok = nc.dram_tensor("sum_tok", [NT, D], BF16).ap()

    with ExitStack() as ctx:
        E = ctx.enter_context
        sems = {n: E(nc.semaphore(n)) for n in
                ("s_sync", "s_gpsimd", "s_scalar", "s_vector", "s_tensor",
                 "d_sync", "d_gpsimd", "d_scalar")}
        ident_sb = E(nc.sbuf_tensor("ident_sb", [128, 128], BF16))
        with ExitStack() as c1:
            ctx_sb = c1.enter_context(nc.sbuf_tensor("ctx_sb", [128, 32 * NT], BF16))
            hg_sb = c1.enter_context(nc.sbuf_tensor("hg_sb", [128, 8 * NT], BF16))
            em = Em()
            em.dma("gpsimd", out=ident_sb[:, :], in_=ident)
            m1 = em.dma("gpsimd", out=ctx_sb.ap().rearrange("p (d n) -> p d n", n=NT),
                        in_=ctxt.rearrange("(d p) n -> p d n", p=128))
            m2 = em.dma("gpsimd", out=hg_sb.ap().rearrange("p (d n) -> p d n", n=NT),
                        in_=hgt.rearrange("(d p) n -> p d n", p=128))
            em.wait("tensor", m1)
            em.wait("tensor", m2)
            emit_back_sum(nc, c1, em,
                          [(wo, 32, ctx_sb), (ball, 8, hg_sb)],
                          sum_tok, ident_sb, tag="c1")
            em.wait("sync", ("d_sync", em.cnt["d_sync"]))
            em.playback(nc, c1, sems)
        with ExitStack() as c2:
            em = Em(carry=em.cnt)
            emit_residual(nc, c2, em, x1, [x, sum_tok], tag="c2")
            em.wait("sync", ("d_sync", em.cnt["d_sync"]))
            em.playback(nc, c2, sems)
    return nc


def build_F():
    nc = bass.Bass("TRN2", target_bir_lowering=False)
    x1 = nc.dram_tensor("x1", [NT, D], F32, kind="ExternalInput").ap()
    ctxt = nc.dram_tensor("ctxt", [D, NT], BF16, kind="ExternalInput").ap()
    xnt_in = nc.dram_tensor("xnt_in", [D, NT], BF16, kind="ExternalInput").ap()
    ztt = nc.dram_tensor("ztt", [E1 * R1, NT], BF16, kind="ExternalInput").ap()
    wo = nc.dram_tensor("wo", [32, 128, D], F32, kind="ExternalInput").ap()
    wup = nc.dram_tensor("wup", [32, 128, D], F32, kind="ExternalInput").ap()
    lb = nc.dram_tensor("lb", [32, 32, 128], F32, kind="ExternalInput").ap()
    bo = nc.dram_tensor("bo", [128, 32], F32, kind="ExternalInput").ap()
    ident = nc.dram_tensor("ident", [128, 128], BF16, kind="ExternalInput").ap()
    out = nc.dram_tensor("out", [NT, D], F32, kind="ExternalOutput").ap()
    sum1_tok = nc.dram_tensor("sum1_tok", [NT, D], BF16).ap()
    sum2_tok = nc.dram_tensor("sum2_tok", [NT, D], BF16).ap()

    with ExitStack() as ctx:
        E = ctx.enter_context
        sems = {n: E(nc.semaphore(n)) for n in
                ("s_sync", "s_gpsimd", "s_scalar", "s_vector", "s_tensor",
                 "d_sync", "d_gpsimd", "d_scalar")}
        ident_sb = E(nc.sbuf_tensor("ident_sb", [128, 128], BF16))
        bo_sb = E(nc.sbuf_tensor("bo_sb", [128, 32], F32))
        with ExitStack() as f1:
            big_sb = f1.enter_context(nc.sbuf_tensor("big_sb", [128, 32 * NT], BF16))
            em = Em()
            em.dma("gpsimd", out=ident_sb[:, :], in_=ident)
            em.dma("gpsimd", out=bo_sb[:, :], in_=bo)
            m1 = em.dma("gpsimd", out=big_sb.ap().rearrange("p (d n) -> p d n", n=NT),
                        in_=ctxt.rearrange("(d p) n -> p d n", p=128))
            em.wait("tensor", m1)
            emit_back_sum(nc, f1, em, [(wo, 32, big_sb)], sum1_tok, ident_sb,
                          bo_sb=bo_sb, tag="f1")
            em.wait("sync", ("d_sync", em.cnt["d_sync"]))
            em.playback(nc, f1, sems)
        with ExitStack() as f2:
            big2_sb = f2.enter_context(nc.sbuf_tensor("big2_sb", [128, 32 * NT], BF16))
            zt_sb = f2.enter_context(nc.sbuf_tensor("zt_sb", [32, NT], BF16))
            em = Em(carry=em.cnt)
            m1 = em.dma("gpsimd", out=big2_sb.ap().rearrange("p (d n) -> p d n", n=NT),
                        in_=xnt_in.rearrange("(d p) n -> p d n", p=128))
            m2 = em.dma("gpsimd", out=zt_sb[:, :], in_=ztt)
            em.wait("tensor", m1)
            em.wait("tensor", m2)
            emit_back_sum(nc, f2, em, [(wup, 32, big2_sb), (lb, "z32", zt_sb)],
                          sum2_tok, ident_sb, tag="f2")
            em.wait("sync", ("d_sync", em.cnt["d_sync"]))
            em.playback(nc, f2, sems)
        with ExitStack() as f3:
            em = Em(carry=em.cnt)
            emit_residual(nc, f3, em, out, [x1, sum1_tok, sum2_tok], tag="f3")
            em.wait("sync", ("d_sync", em.cnt["d_sync"]))
            em.playback(nc, f3, sems)
    return nc




def build_D():
    nc = bass.Bass("TRN2", target_bir_lowering=False)
    x = nc.dram_tensor("x", [NT, D], F32, kind="ExternalInput").ap()
    wq = nc.dram_tensor("wq", [32, 128, D], F32, kind="ExternalInput").ap()
    wk = nc.dram_tensor("wk", [8, 128, D], F32, kind="ExternalInput").ap()
    wv = nc.dram_tensor("wv", [D, KVD], F32, kind="ExternalInput").ap()
    gwt = nc.dram_tensor("gwt", [D, E1], F32, kind="ExternalInput").ap()
    ebias = nc.dram_tensor("ebias", [1, E1], F32, kind="ExternalInput").ap()
    la = nc.dram_tensor("la", [D, E1 * R1], F32, kind="ExternalInput").ap()
    e4 = nc.dram_tensor("e4", [E1, E1 * R1], BF16, kind="ExternalInput").ap()
    ident = nc.dram_tensor("ident", [128, 128], BF16, kind="ExternalInput").ap()
    oner = nc.dram_tensor("oner", [1, 128], F32, kind="ExternalInput").ap()

    q_t = nc.dram_tensor("q_t", [D, NT], BF16, kind="ExternalOutput").ap()
    k_t = nc.dram_tensor("k_t", [KVD, NT], BF16, kind="ExternalOutput").ap()
    v_tok = nc.dram_tensor("v_tok", [NT, KVD], BF16, kind="ExternalOutput").ap()
    zt_t = nc.dram_tensor("zt_t", [E1 * R1, NT], BF16, kind="ExternalOutput").ap()
    xn_t = nc.dram_tensor("xn_t", [D, NT], BF16, kind="ExternalOutput").ap()

    with ExitStack() as ctx:
        E = ctx.enter_context
        sems = {n: E(nc.semaphore(n)) for n in
                ("s_sync", "s_gpsimd", "s_scalar", "s_vector", "s_tensor",
                 "d_sync", "d_gpsimd", "d_scalar")}
        xnt = E(nc.sbuf_tensor("xnt", [128, 32 * NT], BF16))
        ident_sb = E(nc.sbuf_tensor("ident_sb", [128, 128], BF16))
        oner_sb = E(nc.sbuf_tensor("oner_sb", [1, 128], F32))

        with ExitStack() as c1:
            em = Em()
            em.dma("gpsimd", out=oner_sb[:, :], in_=oner)
            m_cst = em.dma("gpsimd", out=ident_sb[:, :], in_=ident)
            em.wait("tensor", m_cst)
            emit_ln_to_xnt(nc, c1, em, x, xnt, ident_sb)
            em.playback(nc, c1, sems)

        with ExitStack() as c2:
            E2 = c2.enter_context
            em = Em(carry=em.cnt)
            stage = [E2(nc.sbuf_tensor(f"qst{i}", [128, NT], BF16)) for i in range(2)]
            st_out_ms = {}
            cp_ms = {}

            def evac_qk(c, qb, ps, m_mm):
                lin = c * 4 + qb
                dma_guard = st_out_ms.get(c - 2) if qb == 0 else None
                m_cp = em.op("scalar", "copy", stage[c % 2][:, qb * 512:(qb + 1) * 512],
                             ps[:, :], after=[m_mm, dma_guard])
                cp_ms[lin] = m_cp
                if qb == 3:
                    dst = q_t[c * 128:(c + 1) * 128, :] if c < 32 else \
                        k_t[(c - 32) * 128:(c - 31) * 128, :]
                    st_out_ms[c] = em.dma("sync", out=dst, in_=stage[c % 2][:, :],
                                          after=[m_cp])
                return m_cp

            emit_featmajor_proj(nc, c2, em, wq, list(range(32)), xnt, evac_qk, "wq")

            def evac_k(c, qb, ps, m_mm):
                return evac_qk(c + 32, qb, ps, m_mm)
            emit_featmajor_proj(nc, c2, em, wk, list(range(8)), xnt, evac_k, "wk")
            em.playback(nc, c2, sems)

        with ExitStack() as c3:
            E3 = c3.enter_context
            em = Em(carry=em.cnt)
            wv_sb = E3(nc.sbuf_tensor("wv_sb", [128, 32 * 512], BF16))
            vstage = [E3(nc.sbuf_tensor(f"vst{i}", [128, 512], BF16)) for i in range(2)]
            pv = [E3(nc.psum_tensor(f"pv{i}", [128, 512], F32)) for i in range(2)]
            last_mm = None
            for half in range(2):
                m_w = em.dma("gpsimd",
                             out=wv_sb.ap().rearrange("p (d j) -> p d j", j=512),
                             in_=wv[:, half * 512:(half + 1) * 512]
                             .rearrange("(d p) j -> p d j", p=128),
                             after=[last_mm])
                ev = {}
                for tt in range(NTT):
                    u = tt % 2
                    for d in range(32):
                        mm = em.op("tensor", "matmul", pv[u][:, :],
                                   xnt[:, d * NT + tt * 128: d * NT + (tt + 1) * 128],
                                   wv_sb[:, d * 512:(d + 1) * 512],
                                   start=(d == 0), stop=(d == 31),
                                   after=([m_w, ev.get(tt - 2)] if d == 0 else ()),
                                   inc=(d == 31))
                    last_mm = mm
                    m_cp = em.op("scalar", "copy", vstage[u][:, :], pv[u][:, :],
                                 after=[mm])
                    ev[tt] = em.dma("sync", out=v_tok[tt * 128:(tt + 1) * 128,
                                                     half * 512:(half + 1) * 512],
                                    in_=vstage[u][:, :], after=[m_cp])
            em.playback(nc, c3, sems)

        with ExitStack() as c4:
            E4 = c4.enter_context
            em = Em(carry=em.cnt)
            gw_sb = E4(nc.sbuf_tensor("gw_sb", [128, 32 * E1], BF16))
            eb_sb = E4(nc.sbuf_tensor("eb_sb", [1, E1], F32))
            la_sb = E4(nc.sbuf_tensor("la_sb", [128, 32 * 32], BF16))
            e4_sb = E4(nc.sbuf_tensor("e4_sb", [E1, E1 * R1], BF16))
            lg = E4(nc.sbuf_tensor("lg", [128, E1], F32))
            mxv = E4(nc.sbuf_tensor("mxv", [128, 1], F32))
            oh = E4(nc.sbuf_tensor("oh", [128, E1], BF16))
            oht = E4(nc.sbuf_tensor("oht", [E1, NT], BF16))
            ztsb = E4(nc.sbuf_tensor("ztsb", [32, NT], BF16))
            ztg = [E4(nc.sbuf_tensor(f"ztg{i}", [32, 512], BF16)) for i in range(2)]
            plg = E4(nc.psum_tensor("plg", [128, E1], F32))
            poh = E4(nc.psum_tensor("poh", [E1, 128], BF16))
            pzt = [E4(nc.psum_tensor(f"pzt{i}", [32, 512], F32)) for i in range(2)]
            pg4 = [E4(nc.psum_tensor(f"pg4{i}", [32, 512], F32)) for i in range(2)]

            m_gw = em.dma("gpsimd",
                          out=gw_sb.ap().rearrange("p (d e) -> p d e", e=E1),
                          in_=gwt.rearrange("(d p) e -> p d e", p=128))
            m_eb = em.dma("gpsimd", out=eb_sb[:, :], in_=ebias)
            m_la = em.dma("gpsimd",
                          out=la_sb.ap().rearrange("p (d e) -> p d e", e=32),
                          in_=la.rearrange("(d p) e -> p d e", p=128))
            m_e4 = em.dma("gpsimd", out=e4_sb[:, :], in_=e4)

            for cchunk in range(32):
                em.dma("sync", out=xn_t[cchunk * 128:(cchunk + 1) * 128, :],
                       in_=xnt[:, cchunk * NT:(cchunk + 1) * NT])

            oht_ms = {}
            for tt in range(NTT):
                for d in range(32):
                    mm = em.op("tensor", "matmul", plg[:, :],
                               xnt[:, d * NT + tt * 128: d * NT + (tt + 1) * 128],
                               gw_sb[:, d * E1:(d + 1) * E1],
                               start=(d == 0), stop=False,
                               after=([m_gw, m_eb, oht_ms.get(tt - 1)] if d == 0 else ()))
                mm = em.op("tensor", "matmul", plg[:, :], oner_sb[:, :], eb_sb[:, :],
                           start=False, stop=True, inc=True)
                m_lg = em.op("vector", "tensor_copy", lg[:, :], plg[:, :], after=[mm])
                em.op("vector", "reduce_max", mxv[:, :], lg[:, :], axis=AX.X)
                m_oh = em.op("vector", "tensor_single_scalar", oh[:, :], lg[:, :],
                             mxv[:, :], op=ALU.is_equal)
                m_tr = em.op("tensor", "transpose", poh[:, :], oh[:, :],
                             ident_sb[:, :], after=[m_oh, oht_ms.get(tt - 1)],
                             inc=True)
                oht_ms[tt] = em.op("vector", "tensor_copy",
                                   oht[0:E1, tt * 128:(tt + 1) * 128], poh[:, :],
                                   after=[m_tr])
            zt_ms = {}
            for qb in range(4):
                u = qb % 2
                for d in range(32):
                    mm = em.op("tensor", "matmul", pzt[u][:, :],
                               la_sb[:, d * 32:(d + 1) * 32].transpose([0, 1]),
                               xnt[:, d * NT + qb * 512: d * NT + qb * 512 + 512],
                               start=(d == 0), stop=(d == 31),
                               after=([m_la, zt_ms.get(qb - 2)] if d == 0 else ()),
                               inc=(d == 31))
                m_cp = em.op("scalar", "copy", ztsb[0:32, qb * 512:(qb + 1) * 512],
                             pzt[u][:, :], after=[mm])
                mmg = em.op("tensor", "matmul", pg4[u][:, :], e4_sb[:, :],
                            oht[0:E1, qb * 512:(qb + 1) * 512],
                            start=True, stop=True,
                            after=[m_e4, oht_ms[NTT - 1], zt_ms.get(qb - 2)],
                            inc=True)
                m_ztg = em.op("vector", "tensor_tensor", ztg[u][:, :], pg4[u][:, :],
                              ztsb[0:32, qb * 512:(qb + 1) * 512], op=ALU.mult,
                              after=[mmg, m_cp])
                zt_ms[qb] = em.dma("sync", out=zt_t[:, qb * 512:(qb + 1) * 512],
                                   in_=ztg[u][:, :], after=[m_ztg])
            for dn in ("d_sync", "d_gpsimd"):
                if em.cnt.get(dn, 0) > 0:
                    em.wait(dn[2:], (dn, em.cnt[dn]))
            em.playback(nc, c4, sems)
    return nc


def run_D(ii, x1):
    nc = build_D()
    in_maps = []
    for c in range(NCORE):
        in_maps.append(dict(
            x=x1[c * NT:(c + 1) * NT],
            wq=ii["wq1"], wk=ii["wk1"], wv=ii["wv1"],
            gwt=ii["gwt1"], ebias=ii["ebias1"], la=ii["la1p"], e4=ii["e4"],
            ident=ii["ident"], oner=ii["ones_row"],
        ))
    res = run_bass_kernel_spmd(nc, in_maps, core_ids=list(range(NCORE)))
    return res.results




BASE_T = ((0, 3), (1, 2))


def stripe_tiles(half):
    return [4 * p + BASE_T[half][r] for p in range(8) for r in range(2)]


def _mask_table(consts, half):
    mv, S512 = consts["maskvar"], 512
    zeros = np.zeros((128, S512), BF)
    full = np.full((128, S512), np.float32(-1e9)).astype(BF)

    def mfull(nf):
        m = np.zeros((128, S512), np.float32)
        m[:, :nf * 128] = -1e9
        return m.astype(BF)

    G = [BASE_T[half][r] + 4 * p for p in range(2) for r in range(2)]
    out = []
    for dd in range(8):
        if dd in G:
            out.append(np.asarray(mv[G.index(dd)]))
        else:
            nf = sum(1 for x in G if x < dd)
            if nf == 0:
                out.append(zeros)
            elif nf >= 4:
                out.append(full)
            else:
                out.append(mfull(nf))
    return np.ascontiguousarray(np.stack(out))


def _emit_qkv_rope(nc, ctx, em, wq, wk, xnt, q_dst, k_dst, cos_sb, sin_sb, rt_sb,
                   rope, tag):
    E = ctx.enter_context
    stage = [E(nc.sbuf_tensor(f"{tag}qst{i}", [128, NT], BF16)) for i in range(2)]
    st_out_ms = {}
    if rope:
        qraw = [E(nc.sbuf_tensor(f"{tag}qraw{i}", [128, 512], BF16)) for i in range(2)]
        t1 = [E(nc.sbuf_tensor(f"{tag}t1_{i}", [128, 512], F32)) for i in range(2)]
        rot = [E(nc.psum_tensor(f"{tag}rot{i}", [128, 512], F32)) for i in range(2)]
        rope_add = {}
        rope_mm = {}

        def evac(c, qb, ps, m_mm):
            lin = c * 4 + qb
            dma_guard = st_out_ms.get(c - 2) if qb == 0 else None
            m_qraw = em.op("scalar", "copy", qraw[qb % 2][:, :], ps[:, :],
                           after=[m_mm, rope_add.get(lin - 2), rope_mm.get(lin - 2),
                                  dma_guard])
            m_rot = em.op("tensor", "matmul", rot[qb % 2][:, :], rt_sb[:, :],
                          qraw[qb % 2][:, :], start=True, stop=True,
                          after=[m_qraw, rope_add.get(lin - 2)], inc=True)
            rope_mm[lin] = m_rot
            em.op("vector", "tensor_tensor", t1[qb % 2][:, :], qraw[qb % 2][:, :],
                  cos_sb[:, qb * 512:(qb + 1) * 512], op=ALU.mult, after=[m_qraw])
            em.op("vector", "scalar_tensor_tensor",
                  stage[c % 2][:, qb * 512:(qb + 1) * 512],
                  rot[qb % 2][:, :], 1.0, sin_sb[:, qb * 512:(qb + 1) * 512],
                  op0=ALU.mult, op1=ALU.mult, after=[m_rot])
            m_add = em.op("vector", "tensor_tensor",
                          stage[c % 2][:, qb * 512:(qb + 1) * 512],
                          stage[c % 2][:, qb * 512:(qb + 1) * 512],
                          t1[qb % 2][:, :], op=ALU.add)
            rope_add[lin] = m_add
            if qb == 3:
                dst = q_dst[c * 128:(c + 1) * 128, :] if c < 32 else \
                    k_dst[(c - 32) * 128:(c - 31) * 128, :]
                st_out_ms[c] = em.dma("sync", out=dst, in_=stage[c % 2][:, :],
                                      after=[m_add])
            return m_qraw
    else:
        def evac(c, qb, ps, m_mm):
            dma_guard = st_out_ms.get(c - 2) if qb == 0 else None
            m_cp = em.op("scalar", "copy", stage[c % 2][:, qb * 512:(qb + 1) * 512],
                         ps[:, :], after=[m_mm, dma_guard])
            if qb == 3:
                dst = q_dst[c * 128:(c + 1) * 128, :] if c < 32 else \
                    k_dst[(c - 32) * 128:(c - 31) * 128, :]
                st_out_ms[c] = em.dma("sync", out=dst, in_=stage[c % 2][:, :],
                                      after=[m_cp])
            return m_cp

    pj_sh = [ctx.enter_context(nc.psum_tensor(f"{tag}pj{i}", [128, 512], F32))
             for i in range(2)]
    _, wq_free = emit_featmajor_proj(nc, ctx, em, wq, list(range(32)), xnt, evac,
                                     tag + "wq", pj=pj_sh)

    def evac_k(c, qb, ps, m_mm):
        return evac(c + 32, qb, ps, m_mm)
    emit_featmajor_proj(nc, ctx, em, wk, list(range(8)), xnt, evac_k, tag + "wk",
                        pj=pj_sh, ready=wq_free[(31, 3)])
    return st_out_ms


def _emit_vproj(nc, ctx, em, wv, xnt, v_dst, tag):
    E = ctx.enter_context
    wv_sb = E(nc.sbuf_tensor(f"{tag}wv_sb", [128, 32 * 512], BF16))
    vstage = [E(nc.sbuf_tensor(f"{tag}vst{i}", [128, 512], BF16)) for i in range(2)]
    pv = [E(nc.psum_tensor(f"{tag}pv{i}", [128, 512], F32)) for i in range(2)]
    last_mm = None
    for half in range(2):
        m_w = em.dma("gpsimd",
                     out=wv_sb.ap().rearrange("p (d j) -> p d j", j=512),
                     in_=wv[:, half * 512:(half + 1) * 512]
                     .rearrange("(d p) j -> p d j", p=128), after=[last_mm])
        ev = {}
        for tt in range(NTT):
            u = tt % 2
            for d in range(32):
                mm = em.op("tensor", "matmul", pv[u][:, :],
                           xnt[:, d * NT + tt * 128: d * NT + (tt + 1) * 128],
                           wv_sb[:, d * 512:(d + 1) * 512],
                           start=(d == 0), stop=(d == 31),
                           after=([m_w, ev.get(tt - 2)] if d == 0 else ()),
                           inc=(d == 31))
            last_mm = mm
            m_cp = em.op("scalar", "copy", vstage[u][:, :], pv[u][:, :], after=[mm])
            ev[tt] = em.dma("sync", out=v_dst[tt * 128:(tt + 1) * 128,
                                             half * 512:(half + 1) * 512],
                            in_=vstage[u][:, :], after=[m_cp])
    return last_mm


def _emit_attn_fused(nc, ctx, em, causal, q_src, agk, agv, ctx_sb,
                     ident_sb, mall_sb, onec_sb, oner_sb, shift_sb, tag):
    E = ctx.enter_context
    kg = [E(nc.sbuf_tensor(f"{tag}kg{i}", [128, S], BF16)) for i in range(2)]
    vg = [E(nc.sbuf_tensor(f"{tag}vg{i}", [128, 32 * 128], BF16)) for i in range(2)]
    qh = [E(nc.sbuf_tensor(f"{tag}qh{i}", [128, NT], BF16)) for i in range(2)]
    p_sb = [E(nc.sbuf_tensor(f"{tag}p{i}", [128, 512], BF16)) for i in range(2)]
    acc = E(nc.sbuf_tensor(f"{tag}acc", [128, 512], F32))
    rsig = E(nc.sbuf_tensor(f"{tag}rsig", [1, 512], F32))
    r_sb = E(nc.sbuf_tensor(f"{tag}r_sb", [128, 512], F32))
    ps_s = [E(nc.psum_tensor(f"{tag}ps_s{i}", [128, 512], F32)) for i in range(2)]
    ps_c = E(nc.psum_tensor(f"{tag}ps_c", [128, 512], F32))
    ps_sig = E(nc.psum_tensor(f"{tag}ps_sig", [1, 512], F32))
    ps_r = E(nc.psum_tensor(f"{tag}ps_r", [128, 512], F32))

    kload, vload, qload = {}, {}, {}
    last_k_use, last_v_use, last_q_use = {}, {}, {}
    evac_ms, sig_ms, rcp_ms = {}, {}, {}
    unit = 0
    for g in range(8):
        for rh in range(2):
            for rr in range(2):
                base = BASE_T[rh][rr]
                km = em.dma(
                    "sync",
                    out=kg[g % 2].ap().rearrange("k (p four c) -> k p four c",
                                                 four=4, c=128)[:, :, base:base + 1, :],
                    in_=agk[rh * 1024 + g * 128: rh * 1024 + (g + 1) * 128, :]
                    .rearrange("k (p two c) -> k p two c", two=2, c=128)
                    [:, :, rr:rr + 1, :],
                    after=[last_k_use.get(g - 2)] if (rh == 0 and rr == 0) else ())
                vm = em.dma(
                    "gpsimd",
                    out=vg[g % 2].ap().rearrange("k (p four d) -> k p four d",
                                                 four=4, d=128)[:, :, base:base + 1, :],
                    in_=agv[rh * NT: (rh + 1) * NT, g * 128:(g + 1) * 128]
                    .rearrange("(p two k) d -> k p two d", two=2, k=128)
                    [:, :, rr:rr + 1, :],
                    after=[last_v_use.get(g - 2)] if (rh == 0 and rr == 0) else ())
        kload[g] = km
        vload[g] = vm
        for h in range(4):
            H = g * 4 + h
            gh = H
            qload[gh] = em.dma("sync", out=qh[gh % 2][:, :],
                               in_=q_src[H * 128:(H + 1) * 128, :],
                               after=[last_q_use.get(gh - 2)])
            for qb in range(4):
                kts = list(range(min(32, 8 * qb + 8))) if causal else list(range(32))
                nk = len(kts)
                m_ms = em.op("vector", "memset", acc[:, :], 0.0,
                             after=[sig_ms.get(unit - 1)])
                exp_ms = {}
                mm_ms = {}
                av_last = None
                for i in range(nk + 1):
                    if i < nk:
                        kt = kts[i]
                        u = i % 2
                        deps = [exp_ms.get(i - 2)]
                        if i == 0:
                            deps += [kload[g], qload[gh], m_ms]
                        dd = kt - 8 * qb
                        masked = causal and dd >= 0
                        mm = em.op("tensor", "matmul", ps_s[u][:, :],
                                   kg[g % 2][:, kt * 128:(kt + 1) * 128],
                                   qh[gh % 2][:, qb * 512:(qb + 1) * 512],
                                   start=True, stop=not masked,
                                   after=deps, inc=not masked)
                        if masked:
                            mm = em.op("tensor", "matmul", ps_s[u][:, :],
                                       ident_sb[:, :],
                                       mall_sb[:, dd * 512:(dd + 1) * 512],
                                       start=False, stop=True, inc=True)
                        mm_ms[i] = mm
                        exp_ms[i] = em.op("scalar", "activation", p_sb[u][:, :],
                                          ps_s[u][:, :], AF.Exp,
                                          bias=shift_sb[:, :], scale=ATT_SCALE,
                                          after=[mm])
                    if i >= 1:
                        j = i - 1
                        u = j % 2
                        deps = [exp_ms[j]]
                        if j == 0:
                            deps += [evac_ms.get(unit - 1), vload[g]]
                        av = em.op("tensor", "matmul", ps_c[:, :],
                                   vg[g % 2][:, kts[j] * 128:(kts[j] + 1) * 128],
                                   p_sb[u][:, :],
                                   start=(j == 0), stop=(j == nk - 1),
                                   after=deps, inc=(j == nk - 1))
                        av_last = av
                        m_sadd = em.op("vector", "tensor_tensor", acc[:, :],
                                       acc[:, :], p_sb[u][:, :], op=ALU.add,
                                       after=[exp_ms[j]])
                last_k_use[g] = mm_ms[nk - 1]
                last_q_use[gh] = mm_ms[nk - 1]
                last_v_use[g] = av_last
                m_sig = em.op("tensor", "matmul", ps_sig[:, :], onec_sb[:, :],
                              acc[:, :], start=True, stop=True,
                              after=[m_sadd, rcp_ms.get(unit - 1)], inc=True)
                sig_ms[unit] = m_sig
                m_rec = em.op("vector", "reciprocal", rsig[:, :], ps_sig[:, :],
                              after=[m_sig])
                m_rmm = em.op("tensor", "matmul", ps_r[:, :], oner_sb[:, :],
                              rsig[:, :], start=True, stop=True,
                              after=[m_rec], inc=True)
                rcp_ms[unit] = em.op("vector", "tensor_copy", r_sb[:, :],
                                     ps_r[:, :], after=[m_rmm])
                evac_ms[unit] = em.op("vector", "tensor_tensor",
                                      ctx_sb[:, H * NT + qb * 512:
                                             H * NT + qb * 512 + 512],
                                      ps_c[:, :], r_sb[:, :], op=ALU.mult,
                                      after=[av_last])
                unit += 1
    return evac_ms[unit - 1]


def _emit_gates_l0(nc, ctx, em, wrt, e64, xnt, hg_dst, aall, ident_sb, tag="g0"):
    E = ctx.enter_context
    wr_sb = E(nc.sbuf_tensor(f"{tag}wr", [128, 32 * 64], BF16))
    e64_sb = E(nc.sbuf_tensor(f"{tag}e64", [64, 1024], BF16))
    sc = E(nc.sbuf_tensor(f"{tag}sc", [128, 64], F32))
    work = E(nc.sbuf_tensor(f"{tag}work", [128, 64], F32))
    eqm = E(nc.sbuf_tensor(f"{tag}eqm", [128, 64], F32))
    ex = E(nc.sbuf_tensor(f"{tag}ex", [128, 64], F32))
    mx = E(nc.sbuf_tensor(f"{tag}mx", [128, 18], F32))
    gtok = E(nc.sbuf_tensor(f"{tag}gtok", [128, 64], BF16))
    gt_sb = E(nc.sbuf_tensor(f"{tag}gt", [64, NT], BF16))
    gexp = E(nc.sbuf_tensor(f"{tag}gexp", [128, 8 * NT], BF16))
    hstage = [E(nc.sbuf_tensor(f"{tag}hst{i}", [128, 512], BF16)) for i in range(2)]
    psc = E(nc.psum_tensor(f"{tag}psc", [128, 64], F32))
    pgt = E(nc.psum_tensor(f"{tag}pgt", [64, 128], BF16))
    pge = [E(nc.psum_tensor(f"{tag}pge0", [128, 512], F32))]

    m_wr = em.dma("gpsimd", out=wr_sb.ap().rearrange("p (d e) -> p d e", e=64),
                  in_=wrt.rearrange("(d p) e -> p d e", p=128))
    m_e64 = em.dma("gpsimd", out=e64_sb[:, :], in_=e64)
    gt_ms = {}
    for tt in range(NTT):
        for d in range(32):
            mm = em.op("tensor", "matmul", psc[:, :],
                       xnt[:, d * NT + tt * 128: d * NT + (tt + 1) * 128],
                       wr_sb[:, d * 64:(d + 1) * 64],
                       start=(d == 0), stop=(d == 31),
                       after=([m_wr, gt_ms.get(tt - 1)] if d == 0 else ()),
                       inc=(d == 31))
        m_sc = em.op("vector", "tensor_copy", sc[:, :], psc[:, :], after=[mm])
        em.op("vector", "tensor_copy", work[:, :], sc[:, :])
        em.op("vector", "reduce_max", mx[:, 0:1], work[:, :], axis=AX.X)
        for i in range(1, TOPK):
            em.op("vector", "tensor_single_scalar", eqm[:, :], work[:, :],
                  mx[:, i - 1:i], op=ALU.is_equal)
            em.op("vector", "scalar_tensor_tensor", work[:, :], eqm[:, :],
                  -1e30, work[:, :], op0=ALU.mult, op1=ALU.add)
            em.op("vector", "reduce_max", mx[:, i:i + 1], work[:, :], axis=AX.X)
        em.op("vector", "tensor_single_scalar", eqm[:, :], sc[:, :],
              mx[:, TOPK - 1:TOPK], op=ALU.is_ge)
        m_negm = em.op("vector", "tensor_scalar_mul", mx[:, 16:17], mx[:, 0:1], -1.0)
        m_exp = em.op("scalar", "activation", ex[:, :], sc[:, :], AF.Exp,
                      bias=mx[:, 16:17], after=[m_negm])
        em.op("vector", "tensor_mul", ex[:, :], ex[:, :], eqm[:, :], after=[m_exp])
        em.op("vector", "reduce_sum", mx[:, 17:18], ex[:, :], axis=AX.X)
        em.op("vector", "reciprocal", mx[:, 17:18], mx[:, 17:18])
        m_g = em.op("vector", "tensor_scalar", gtok[:, :], ex[:, :],
                    mx[:, 17:18], SCALING0, op0=ALU.mult, op1=ALU.mult)
        m_tr = em.op("tensor", "transpose", pgt[:, :], gtok[:, :], ident_sb[:, :],
                     after=[m_g], inc=True)
        gt_ms[tt] = em.op("vector", "tensor_copy",
                          gt_sb[0:64, tt * 128:(tt + 1) * 128], pgt[:, :],
                          after=[m_tr])
    ge_ms = {}
    for ec in range(8):
        for qb in range(4):
            u = 0
            mm = em.op("tensor", "matmul", pge[u][:, :],
                       e64_sb[:, ec * 128:(ec + 1) * 128],
                       gt_sb[0:64, qb * 512:(qb + 1) * 512],
                       start=True, stop=True,
                       after=[m_e64, gt_ms[NTT - 1], ge_ms.get((ec * 4 + qb) - 1)],
                       inc=True)
            ge_ms[ec * 4 + qb] = em.op(
                "vector", "tensor_copy",
                gexp[:, ec * NT + qb * 512: ec * NT + qb * 512 + 512],
                pge[u][:, :], after=[mm])
    m_gexp = ge_ms[31]
    hout_ms = {}

    def evac_h(c, qb, ps, m_mm):
        lin = c * 4 + qb
        u = lin % 2
        m_ev = em.op("vector", "tensor_tensor", hstage[u][:, :], ps[:, :],
                     gexp[:, c * NT + qb * 512: c * NT + qb * 512 + 512],
                     op=ALU.mult, after=[m_mm, m_gexp, hout_ms.get(lin - 2)])
        hout_ms[lin] = em.dma("sync", out=hg_dst[c * 128:(c + 1) * 128,
                                                 qb * 512:(qb + 1) * 512],
                              in_=hstage[u][:, :], after=[m_ev])
        return m_ev

    emit_featmajor_proj(nc, ctx, em, aall, list(range(8)), xnt, evac_h, tag + "ha")
    return hout_ms[31]


def _emit_gates_l1(nc, ctx, em, gwt, ebias, la, e4, xnt, zt_dst, xn_dst,
                   ident_sb, oner_sb, tag="g1"):
    E = ctx.enter_context
    gw_sb = E(nc.sbuf_tensor(f"{tag}gw", [128, 32 * E1], BF16))
    eb_sb = E(nc.sbuf_tensor(f"{tag}eb", [1, E1], F32))
    la_sb = E(nc.sbuf_tensor(f"{tag}la", [128, 32 * 32], BF16))
    e4_sb = E(nc.sbuf_tensor(f"{tag}e4", [E1, E1 * R1], BF16))
    lg = E(nc.sbuf_tensor(f"{tag}lg", [128, E1], F32))
    mxv = E(nc.sbuf_tensor(f"{tag}mx", [128, 1], F32))
    oh = E(nc.sbuf_tensor(f"{tag}oh", [128, E1], BF16))
    oht = E(nc.sbuf_tensor(f"{tag}oht", [E1, NT], BF16))
    ztsb = E(nc.sbuf_tensor(f"{tag}zt", [32, NT], BF16))
    ztg = [E(nc.sbuf_tensor(f"{tag}ztg{i}", [32, 512], BF16)) for i in range(2)]
    plg = E(nc.psum_tensor(f"{tag}plg", [128, E1], F32))
    poh = E(nc.psum_tensor(f"{tag}poh", [E1, 128], BF16))
    pzt = E(nc.psum_tensor(f"{tag}pzt", [32, 512], F32))
    pg4 = E(nc.psum_tensor(f"{tag}pg4", [32, 512], F32))

    m_gw = em.dma("gpsimd", out=gw_sb.ap().rearrange("p (d e) -> p d e", e=E1),
                  in_=gwt.rearrange("(d p) e -> p d e", p=128))
    m_eb = em.dma("gpsimd", out=eb_sb[:, :], in_=ebias)
    m_la = em.dma("gpsimd", out=la_sb.ap().rearrange("p (d e) -> p d e", e=32),
                  in_=la.rearrange("(d p) e -> p d e", p=128))
    m_e4 = em.dma("gpsimd", out=e4_sb[:, :], in_=e4)
    for cchunk in range(32):
        em.dma("sync", out=xn_dst[cchunk * 128:(cchunk + 1) * 128, :],
               in_=xnt[:, cchunk * NT:(cchunk + 1) * NT])
    oht_ms = {}
    for tt in range(NTT):
        for d in range(32):
            em.op("tensor", "matmul", plg[:, :],
                  xnt[:, d * NT + tt * 128: d * NT + (tt + 1) * 128],
                  gw_sb[:, d * E1:(d + 1) * E1],
                  start=(d == 0), stop=False,
                  after=([m_gw, m_eb, oht_ms.get(tt - 1)] if d == 0 else ()))
        mm = em.op("tensor", "matmul", plg[:, :], oner_sb[:, :], eb_sb[:, :],
                   start=False, stop=True, inc=True)
        m_lg = em.op("vector", "tensor_copy", lg[:, :], plg[:, :], after=[mm])
        em.op("vector", "reduce_max", mxv[:, :], lg[:, :], axis=AX.X)
        m_oh = em.op("vector", "tensor_single_scalar", oh[:, :], lg[:, :],
                     mxv[:, :], op=ALU.is_equal)
        m_tr = em.op("tensor", "transpose", poh[:, :], oh[:, :], ident_sb[:, :],
                     after=[m_oh, oht_ms.get(tt - 1)], inc=True)
        oht_ms[tt] = em.op("vector", "tensor_copy",
                           oht[0:E1, tt * 128:(tt + 1) * 128], poh[:, :],
                           after=[m_tr])
    zt_ms = {}
    ztg_ms = {}
    for qb in range(4):
        u = qb % 2
        for d in range(32):
            mm = em.op("tensor", "matmul", pzt[:, :],
                       la_sb[:, d * 32:(d + 1) * 32],
                       xnt[:, d * NT + qb * 512: d * NT + qb * 512 + 512],
                       start=(d == 0), stop=(d == 31),
                       after=([m_la, zt_ms.get(qb - 1)] if d == 0 else ()),
                       inc=(d == 31))
        m_cp = em.op("scalar", "copy", ztsb[0:32, qb * 512:(qb + 1) * 512],
                     pzt[:, :], after=[mm])
        mmg = em.op("tensor", "matmul", pg4[:, :], e4_sb[:, :],
                    oht[0:E1, qb * 512:(qb + 1) * 512], start=True, stop=True,
                    after=[m_e4, oht_ms[NTT - 1], ztg_ms.get(qb - 1)], inc=True)
        m_ztg = em.op("vector", "tensor_tensor", ztg[u][:, :], pg4[:, :],
                      ztsb[0:32, qb * 512:(qb + 1) * 512], op=ALU.mult,
                      after=[mmg, m_cp])
        ztg_ms[qb] = m_ztg
        zt_ms[qb] = em.dma("sync", out=zt_dst[:, qb * 512:(qb + 1) * 512],
                           in_=ztg[u][:, :], after=[m_ztg])
    return zt_ms[3]


def build_fused():
    nc = bass.Bass("TRN2", target_bir_lowering=False)
    T = lambda n, s, d, k="ExternalInput": nc.dram_tensor(n, s, d, kind=k)
    x = T("x", [NT, D], F32).ap()
    wq0 = T("wq0", [32, 128, D], F32).ap()
    wk0 = T("wk0", [8, 128, D], F32).ap()
    wv0 = T("wv0", [D, KVD], F32).ap()
    wo0 = T("wo0", [32, 128, D], F32).ap()
    wrt0 = T("wrt0", [D, KEXP], F32).ap()
    aall0 = T("aall0", [8, 128, D], F32).ap()
    ball0 = T("ball0", [32, 128, KEXP * RANK], F32).ap()
    wq1 = T("wq1", [32, 128, D], F32).ap()
    wk1 = T("wk1", [8, 128, D], F32).ap()
    wv1 = T("wv1", [D, KVD], F32).ap()
    wo1 = T("wo1", [32, 128, D], F32).ap()
    wup1 = T("wup1", [32, 128, D], F32).ap()
    gwt1 = T("gwt1", [D, E1], F32).ap()
    ebias1 = T("ebias1", [1, E1], F32).ap()
    la1 = T("la1", [D, E1 * R1], F32).ap()
    lb1 = T("lb1", [32, 32, 128], F32).ap()
    bo1 = T("bo1", [128, 32], F32).ap()
    cost = T("cost", [128, NT], F32).ap()
    sint = T("sint", [128, NT], F32).ap()
    ident = T("ident", [128, 128], BF16).ap()
    rtm = T("rtm", [128, 128], BF16).ap()
    e64 = T("e64", [64, 1024], BF16).ap()
    e4 = T("e4", [E1, E1 * R1], BF16).ap()
    mall = T("mall", [8, 128, 512], BF16).ap()
    onec = T("onec", [128, 1], F32).ap()
    oner = T("oner", [1, 128], F32).ap()
    out = T("out", [NT, D], F32, "ExternalOutput").ap()

    q_sp = [nc.dram_tensor(f"q_sp{l}", [D, NT], BF16).ap() for l in range(2)]
    kblob = [nc.dram_tensor(f"kblob{l}", [KVD, NT], BF16) for l in range(2)]
    vblob = [nc.dram_tensor(f"vblob{l}", [NT, KVD], BF16) for l in range(2)]
    agk = [nc.dram_tensor(f"agk{l}", [2 * KVD, NT], BF16) for l in range(2)]
    agv = [nc.dram_tensor(f"agv{l}", [2 * NT, KVD], BF16) for l in range(2)]
    hg_t = nc.dram_tensor("hg_t", [KEXP * RANK, NT], BF16).ap()
    xn1_t = nc.dram_tensor("xn1_t", [D, NT], BF16).ap()
    zt_t = nc.dram_tensor("zt_t", [E1 * R1, NT], BF16).ap()
    x1 = nc.dram_tensor("x1", [NT, D], F32).ap()
    s_tok = [nc.dram_tensor(f"s_tok{i}", [NT, D], BF16).ap() for i in range(2)]

    with ExitStack() as ctx:
        E = ctx.enter_context
        sems = {n: E(nc.semaphore(n)) for n in
                ("s_sync", "s_gpsimd", "s_scalar", "s_vector", "s_tensor",
                 "d_sync", "d_gpsimd", "d_scalar", "c_cc")}
        ident_sb = E(nc.sbuf_tensor("ident_sb", [128, 128], BF16))
        mall_sb = E(nc.sbuf_tensor("mall_sb", [128, 8 * 512], BF16))
        onec_sb = E(nc.sbuf_tensor("onec_sb", [128, 1], F32))
        oner_sb = E(nc.sbuf_tensor("oner_sb", [1, 128], F32))
        shift_sb = E(nc.sbuf_tensor("shift_sb", [128, 1], F32))
        cnt = None
        for layer in range(2):
            x_in = x if layer == 0 else x1
            x_out = x1 if layer == 0 else out
            with ExitStack() as cf:
                xnt = cf.enter_context(
                    nc.sbuf_tensor(f"xnt{layer}", [128, 32 * NT], BF16))
                if layer == 0:
                    rope_stack = ExitStack()
                    rt_sb = rope_stack.enter_context(
                        nc.sbuf_tensor("rt_sb", [128, 128], BF16))
                    cos_sb = rope_stack.enter_context(
                        nc.sbuf_tensor("cos_sb", [128, NT], F32))
                    sin_sb = rope_stack.enter_context(
                        nc.sbuf_tensor("sin_sb", [128, NT], F32))
                else:
                    rt_sb = cos_sb = sin_sb = None
                with ExitStack() as c1ln:
                    em = Em(carry=cnt)
                    if layer == 0:
                        em.dma("gpsimd", out=ident_sb[:, :], in_=ident)
                        em.dma("gpsimd", out=rt_sb[:, :], in_=rtm)
                        em.dma("gpsimd", out=cos_sb[:, :], in_=cost)
                        em.dma("gpsimd", out=sin_sb[:, :], in_=sint)
                        em.dma("gpsimd",
                               out=mall_sb.ap().rearrange("p (d j) -> p d j", j=512),
                               in_=mall.rearrange("d p j -> p d j"))
                        em.dma("gpsimd", out=onec_sb[:, :], in_=onec)
                        m_cst = em.dma("gpsimd", out=oner_sb[:, :], in_=oner)
                        em.op("vector", "memset", shift_sb[:, :], ATT_SHIFT)
                        em.wait("tensor", m_cst)
                    emit_ln_to_xnt(nc, c1ln, em, x_in, xnt, ident_sb, tag=f"ln{layer}")
                    em.playback(nc, c1ln, sems)
                    cnt = em.cnt

                with ExitStack() as c2a:
                    em = Em(carry=cnt)
                    wqL, wkL, wvL = (wq0, wk0, wv0) if layer == 0 else (wq1, wk1, wv1)
                    _emit_qkv_rope(nc, c2a, em, wqL, wkL, xnt, q_sp[layer],
                                   kblob[layer].ap(), cos_sb, sin_sb, rt_sb,
                                   layer == 0, f"p{layer}")
                    em.wait("sync", ("d_sync", em.cnt["d_sync"]))
                    em.playback(nc, c2a, sems)
                    cnt = em.cnt
                if layer == 0:
                    rope_stack.close()
                with ExitStack() as c2b:
                    em = Em(carry=cnt)
                    m_agk = em.op("gpsimd", "collective_compute",
                                  "AllGather", ALU.bypass, sem_override="c_cc",
                                  replica_groups=[[2 * b, 2 * b + 1] for b in range(4)],
                                  ins=[kblob[layer].ap().opt()],
                                  outs=[agk[layer].ap().opt()])
                    _emit_vproj(nc, c2b, em, wvL, xnt, vblob[layer].ap(), f"p{layer}")
                    em.wait("sync", ("d_sync", em.cnt["d_sync"]))
                    em.playback(nc, c2b, sems)
                    cnt = em.cnt
                with ExitStack() as c2c:
                    em = Em(carry=cnt)
                    m_agv = em.op("gpsimd", "collective_compute",
                                  "AllGather", ALU.bypass, sem_override="c_cc",
                                  replica_groups=[[2 * b, 2 * b + 1] for b in range(4)],
                                  ins=[vblob[layer].ap().opt()],
                                  outs=[agv[layer].ap().opt()])
                    if layer == 0:
                        _emit_gates_l0(nc, c2c, em, wrt0, e64, xnt, hg_t, aall0,
                                       ident_sb, tag="g0")
                    else:
                        _emit_gates_l1(nc, c2c, em, gwt1, ebias1, la1, e4, xnt,
                                       zt_t, xn1_t, ident_sb, oner_sb, tag="g1")
                    em.wait("sync", ("d_sync", em.cnt["d_sync"]))
                    em.playback(nc, c2c, sems)
                    cnt = em.cnt

            with ExitStack() as ca:
                ctx_sb = ca.enter_context(
                    nc.sbuf_tensor(f"ctxsb{layer}", [128, 32 * NT], BF16))
                with ExitStack() as ca2:
                    em = Em(carry=cnt)
                    em.wait("sync", m_agk)
                    em.wait("gpsimd", m_agv)
                    _emit_attn_fused(nc, ca2, em, layer == 0, q_sp[layer],
                                     agk[layer].ap(), agv[layer].ap(), ctx_sb,
                                     ident_sb, mall_sb, onec_sb, oner_sb,
                                     shift_sb, tag=f"at{layer}")
                    em.playback(nc, ca2, sems)
                    cnt = em.cnt

                with ExitStack() as cb:
                    em = Em(carry=cnt)
                    if layer == 0:
                        hg_sb = cb.enter_context(
                            nc.sbuf_tensor("hg_sb", [128, 8 * NT], BF16))
                        m1 = em.dma("gpsimd",
                                    out=hg_sb.ap().rearrange("p (d n) -> p d n", n=NT),
                                    in_=hg_t.rearrange("(d p) n -> p d n", p=128))
                        em.wait("tensor", m1)
                        emit_back_sum(nc, cb, em,
                                      [(wo0, 32, ctx_sb), (ball0, 8, hg_sb)],
                                      s_tok[0], ident_sb, tag="b0")
                    else:
                        bo_sb = cb.enter_context(
                            nc.sbuf_tensor("bo_sb", [128, 32], F32))
                        m1 = em.dma("gpsimd", out=bo_sb[:, :], in_=bo1)
                        em.wait("scalar", m1)
                        emit_back_sum(nc, cb, em, [(wo1, 32, ctx_sb)],
                                      s_tok[0], ident_sb, bo_sb=bo_sb, tag="b1")
                    em.wait("sync", ("d_sync", em.cnt["d_sync"]))
                    em.wait("gpsimd", ("d_gpsimd", em.cnt["d_gpsimd"]))
                    em.playback(nc, cb, sems)
                    cnt = em.cnt

            if layer == 1:
                with ExitStack() as cb2:
                    big2 = cb2.enter_context(
                        nc.sbuf_tensor("big2", [128, 32 * NT], BF16))
                    ztsb2 = cb2.enter_context(nc.sbuf_tensor("ztsb2", [32, NT], BF16))
                    em = Em(carry=cnt)
                    m1 = em.dma("gpsimd",
                                out=big2.ap().rearrange("p (d n) -> p d n", n=NT),
                                in_=xn1_t.rearrange("(d p) n -> p d n", p=128))
                    m2 = em.dma("gpsimd", out=ztsb2[:, :], in_=zt_t)
                    em.wait("tensor", m1)
                    em.wait("tensor", m2)
                    emit_back_sum(nc, cb2, em,
                                  [(wup1, 32, big2), (lb1, "z32", ztsb2)],
                                  s_tok[1], ident_sb, tag="b2")
                    em.wait("sync", ("d_sync", em.cnt["d_sync"]))
                    em.wait("gpsimd", ("d_gpsimd", em.cnt["d_gpsimd"]))
                    em.playback(nc, cb2, sems)
                    cnt = em.cnt

            with ExitStack() as cr:
                em = Em(carry=cnt)
                parts = [x_in, s_tok[0]] if layer == 0 else [x_in, s_tok[0], s_tok[1]]
                emit_residual(nc, cr, em, x_out, parts, tag=f"r{layer}")
                em.wait("sync", ("d_sync", em.cnt["d_sync"]))
                em.playback(nc, cr, sems)
                cnt = em.cnt
    return nc


_FUSED_CACHE = {}


def kernel(**inputs):
    ii = _prep_host(inputs)
    tiles = [stripe_tiles(0), stripe_tiles(1)]
    malls = [_mask_table(ii, 0), _mask_table(ii, 1)]
    xg = np.ascontiguousarray(np.asarray(inputs["L0_attn_x"], np.float32)) \
        .reshape(B, 32, 128, D)
    in_maps = []
    for c in range(NCORE):
        b, half = c // 2, c % 2
        xc = np.ascontiguousarray(xg[b][tiles[half]].reshape(NT, D))
        pos = np.concatenate([np.arange(t * 128, (t + 1) * 128) for t in tiles[half]])
        in_maps.append(dict(
            x=xc,
            wq0=ii["wq0"], wk0=ii["wk0"], wv0=ii["wv0"], wo0=ii["wo0"],
            wrt0=ii["wrt0"], aall0=ii["aall0"], ball0=ii["ball0"],
            wq1=ii["wq1"], wk1=ii["wk1"], wv1=ii["wv1"], wo1=ii["wo1"],
            wup1=ii["wup1"], gwt1=ii["gwt1"], ebias1=ii["ebias1"],
            la1=ii["la1p"], lb1=ii["lb1"], bo1=ii["bo1"],
            cost=np.ascontiguousarray(ii["cosT"][:, pos]),
            sint=np.ascontiguousarray(ii["sinT"][:, pos]),
            ident=ii["ident"], rtm=ii["rT"], e64=ii["e64"], e4=ii["e4"],
            mall=malls[half],
            onec=ii["ones_col"], oner=ii["ones_row"],
        ))
    if "nc" not in _FUSED_CACHE:
        _FUSED_CACHE["nc"] = build_fused()
    nc = _FUSED_CACHE["nc"]
    res = run_bass_kernel_spmd(nc, in_maps, core_ids=list(range(NCORE)))
    outg = np.zeros((B, 32, 128, D), np.float32)
    for c in range(NCORE):
        b, half = c // 2, c % 2
        outg[b][tiles[half]] = res.results[c]["out"].reshape(16, 128, D)
    return outg.reshape(B, S, D)



def _prep_host(inputs):
    ii = {}
    ii["x"] = np.ascontiguousarray(inputs["L0_attn_x"].reshape(N, D))
    ii["wq0"] = _chunk_lhsT(np.asarray(inputs["L0_attn_Wq"], np.float32))
    ii["wk0"] = _chunk_lhsT(np.asarray(inputs["L0_attn_Wk"], np.float32))
    ii["wv0"] = np.ascontiguousarray(np.asarray(inputs["L0_attn_Wv"], np.float32))
    ii["wo0"] = _chunk_lhsT(np.asarray(inputs["L0_attn_Wo"], np.float32))
    ii["wrt0"] = np.ascontiguousarray(np.asarray(inputs["L0_ffn_Wr"], np.float32).T)
    ii["aall0"] = _chunk_lhsT(np.ascontiguousarray(
        np.asarray(inputs["L0_ffn_A"], np.float32).reshape(KEXP * RANK, D).T))
    ii["ball0"] = _chunk_lhsT_k(np.ascontiguousarray(
        np.asarray(inputs["L0_ffn_B"], np.float32).transpose(0, 2, 1).reshape(KEXP * RANK, D)))
    ii["wq1"] = _chunk_lhsT(np.ascontiguousarray(np.asarray(inputs["L1_attn_Wq"], np.float32).T))
    ii["wk1"] = _chunk_lhsT(np.ascontiguousarray(np.asarray(inputs["L1_attn_Wk"], np.float32).T))
    ii["wv1"] = np.ascontiguousarray(np.asarray(inputs["L1_attn_Wv"], np.float32).T)
    ii["wo1"] = _chunk_lhsT(np.ascontiguousarray(np.asarray(inputs["L1_attn_Wo"], np.float32).T))
    ii["wup1"] = _chunk_lhsT(np.ascontiguousarray(np.asarray(inputs["L1_ffn_W_up"], np.float32).T))
    ii["gwt1"] = np.ascontiguousarray(np.asarray(inputs["L1_ffn_gate_w"], np.float32).T)
    ii["ebias1"] = np.ascontiguousarray(np.asarray(inputs["L1_ffn_expert_bias"], np.float32).reshape(1, E1))
    ii["la1p"] = np.ascontiguousarray(np.asarray(inputs["L1_ffn_lora_A"], np.float32).reshape(E1 * R1, D).T)
    ii["lb1"] = _chunk_lhsT_k(np.ascontiguousarray(
        np.asarray(inputs["L1_ffn_lora_B"], np.float32).transpose(0, 2, 1).reshape(E1 * R1, D)))
    ii["bo1"] = np.ascontiguousarray(np.asarray(inputs["L1_attn_bo"], np.float32).reshape(32, 128).T)
    ii.update(_consts())
    return ii


def _chunk_lhsT_k(w):
    k, m = w.shape
    if k % 128 == 0:
        return _chunk_lhsT(w)
    return np.ascontiguousarray(w.reshape(k, m // 128, 128).transpose(1, 0, 2))


def run_A(ii):
    nc = build_A(0)
    in_maps = []
    for c in range(NCORE):
        half = c % 2
        in_maps.append(dict(
            x=ii["x"][c * NT:(c + 1) * NT],
            wq=ii["wq0"], wk=ii["wk0"], wv=ii["wv0"], wrt=ii["wrt0"],
            aall=ii["aall0"],
            cost=ii["cosT"][:, half * NT:(half + 1) * NT],
            sint=ii["sinT"][:, half * NT:(half + 1) * NT],
            ident=ii["ident"], rtm=ii["rT"], e64=ii["e64"],
        ))
    res = run_bass_kernel_spmd(nc, in_maps, core_ids=list(range(NCORE)))
    return res.results


def run_C(ii, ra, ctxT):
    nc = build_C()
    in_maps = []
    for c in range(NCORE):
        b, half = c // 2, c % 2
        in_maps.append(dict(
            x=ii["x"][c * NT:(c + 1) * NT],
            ctxt=np.ascontiguousarray(ctxT[b][:, half * NT:(half + 1) * NT]),
            hgt=ra[c]["hg_t"],
            wo=ii["wo0"], ball=ii["ball0"], ident=ii["ident"],
        ))
    res = run_bass_kernel_spmd(nc, in_maps, core_ids=list(range(NCORE)))
    return np.concatenate([res.results[c]["x1"] for c in range(NCORE)], axis=0)


def run_F(ii, x1, rd, ctx1T):
    nc = build_F()
    in_maps = []
    for c in range(NCORE):
        b, half = c // 2, c % 2
        in_maps.append(dict(
            x1=x1[c * NT:(c + 1) * NT],
            ctxt=np.ascontiguousarray(ctx1T[b][:, half * NT:(half + 1) * NT]),
            xnt_in=rd[c]["xn_t"], ztt=rd[c]["zt_t"],
            wo=ii["wo1"], wup=ii["wup1"], lb=ii["lb1"],
            bo=ii["bo1"], ident=ii["ident"],
        ))
    res = run_bass_kernel_spmd(nc, in_maps, core_ids=list(range(NCORE)))
    return np.concatenate([res.results[c]["out"] for c in range(NCORE)], axis=0)


def _assemble_bgh(rr):
    qT, kT, vt = [], [], []
    for b in range(B):
        qT.append(np.concatenate([rr[2 * b]["q_t"], rr[2 * b + 1]["q_t"]], axis=1))
        kT.append(np.concatenate([rr[2 * b]["k_t"], rr[2 * b + 1]["k_t"]], axis=1))
        vt.append(np.concatenate([rr[2 * b]["v_tok"], rr[2 * b + 1]["v_tok"]], axis=0))
    return qT, kT, vt


def kernel_multilaunch(**inputs):
    ii = _prep_host(inputs)
    ra = run_A(ii)
    qT, kT, vt = _assemble_bgh(ra)
    ctxT = run_attn(ii, 0, qT, kT, vt)
    x1 = run_C(ii, ra, ctxT)
    rd = run_D(ii, x1)
    q1, k1, v1 = _assemble_bgh(rd)
    ctx1T = run_attn(ii, 1, q1, k1, v1)
    out = run_F(ii, x1, rd, ctx1T)
    return out.reshape(B, S, D)
